# version 14
# baseline (speedup 1.0000x reference)
import numpy as np
import ml_dtypes
import concourse.bass as bass
import concourse.mybir as mybir
from concourse.bass_utils import run_bass_kernel_spmd

F32 = mybir.dt.float32
BF16 = mybir.dt.bfloat16
AF = mybir.ActivationFunctionType
ALU = mybir.AluOpType
AXL = mybir.AxisListType

P = 128
D = 1024
FH = 512
H = 8
DKH = 64
FF = 4096
CH = 128
ST = 2
EPS_LN = 1e-5
EPS_AT = 1e-6
THS = [0.24419428096993973, 0.4639691338428854, 0.6861859295255306,
       0.9059607823984763, 1.0839784132255623]


class Eng:
    def __init__(self, name, no_drain=False):
        self.name = name
        self.no_drain = no_drain
        self.prog = []
        self.sem = None
        self.count = 0
        self.dirty_w = set()
        self.dirty_r = set()

    def drain(self):
        self.prog.append(lambda e: e.drain())
        self.dirty_w.clear()
        self.dirty_r.clear()

    def op(self, fn, reads=(), writes=(), inc=False):
        hazard = (not self.no_drain) and (
            any(t in self.dirty_w for t in reads) or
            any((t in self.dirty_w or t in self.dirty_r) for t in writes))
        if hazard:
            self.drain()
        if inc:
            self.count += 1
            me = self

            def run(e, fn=fn):
                fn(e).then_inc(me.sem, 1)
            self.prog.append(run)
        else:
            self.prog.append(lambda e, fn=fn: fn(e))
        self.dirty_r.update(reads)
        self.dirty_w.update(writes)
        return self.count

    def dma(self, dst_ap, src_ap, key="d"):
        if not hasattr(self, "dsem"):
            self.dsem = {}
            self.dcnt = {}
        if key not in self.dcnt:
            self.dcnt[key] = 0
            self.dsem[key] = None
        self.dcnt[key] += 1
        me = self

        def run(e, d=dst_ap, s=src_ap, key=key):
            e.dma_start(out=d, in_=s).then_inc(me.dsem[key], 16)
        self.prog.append(run)
        return (key, self.dcnt[key])

    def wait(self, other, value, dma=False):
        if dma:
            key, n = value
            if n <= 0:
                return
            o = other
            self.prog.append(lambda e: e.wait_ge(o.dsem[key], n * 16))
        else:
            if value <= 0:
                return
            o = other
            self.prog.append(lambda e: e.wait_ge(o.sem, value))


def build_program(T):
    nch = T // CH
    nst = nch // ST
    assert nch % ST == 0 and nch >= 4
    nc = bass.Bass("TRN2", target_bir_lowering=False)

    dt_ = nc.dram_tensor
    x_tok_d = dt_("x_tok", [nch, P, D], F32, kind="ExternalInput").ap()
    wq_d = dt_("wq", [P, 8 * FH], F32, kind="ExternalInput").ap()
    wk_d = dt_("wk", [P, 8 * FH], F32, kind="ExternalInput").ap()
    wv_d = dt_("wv", [P, 8 * FH], F32, kind="ExternalInput").ap()
    wg_d = dt_("wg", [P, 8 * FH], F32, kind="ExternalInput").ap()
    wo_d = dt_("wo", [P, 4 * D], F32, kind="ExternalInput").ap()
    fc_d = dt_("fc", [P, 8 * FF], F32, kind="ExternalInput").ap()
    pj_d = dt_("pj", [P, 32 * D], F32, kind="ExternalInput").ap()
    bg_d = dt_("bg", [P, 4], F32, kind="ExternalInput").ap()
    dl_d = dt_("dl", [1, H], F32, kind="ExternalInput").ap()
    identb_d = dt_("identb", [P, P], BF16, kind="ExternalInput").ap()
    tsc_d = dt_("tsc", [P, P], BF16, kind="ExternalInput").ap()
    zmask_d = dt_("zmask", [P, P], BF16, kind="ExternalInput").ap()
    r1_d = dt_("r1", [P, P], BF16, kind="ExternalInput").ap()
    r127_d = dt_("r127", [P, 1], F32, kind="ExternalInput").ap()
    sel_d = dt_("sel", [H, FH], BF16, kind="ExternalInput").ap()
    onesr_d = dt_("onesr", [1, P], F32, kind="ExternalInput").ap()
    onescb_d = dt_("onescb", [P, 1], BF16, kind="ExternalInput").ap()
    out_d = dt_("out", [nch, P, D], F32, kind="ExternalOutput").ap()

    S_ = nc.alloc_sbuf_tensor
    wq = S_("wq_sb", [P, 8 * FH], BF16)
    wk = S_("wk_sb", [P, 8 * FH], BF16)
    wv = S_("wv_sb", [P, 8 * FH], BF16)
    wg = S_("wg_sb", [P, 8 * FH], BF16)
    wo = S_("wo_sb", [P, 4 * D], BF16)
    fc = S_("fc_sb", [P, 8 * FF], BF16)
    pj = S_("pj_sb", [P, 32 * D], BF16)
    XR = S_("XR", [P, 8 * ST * CH], BF16)
    XT = [S_(f"XT{i}", [P, D], BF16) for i in range(2)]
    XF = S_("XF", [P, D], BF16)
    G = S_("G", [P, ST * D], BF16)
    MEM = S_("MEM", [P, D], F32)
    SGN = S_("SGN", [P, FH], BF16)
    SGN2 = S_("SGN2", [P, FH], BF16)
    QF = S_("QF", [P, FH], BF16)
    KF = S_("KF", [P, FH], BF16)
    THF = S_("THF", [P, FH], BF16)
    VT = S_("VT", [P, FH], BF16)
    VAUG = S_("VAUG", [P, H * 65], BF16)
    AT = S_("AT", [P, H * P], BF16)
    ZP = S_("ZP", [P, 2 * FH], BF16)
    SCRB = S_("SCRB", [P, FH], BF16)
    H2 = [S_(f"H2_{i}", [P, ST * CH], BF16) for i in range(2)]
    OUTB = MEM
    SSB = S_("SSB", [P, 4 * 65], BF16)
    GDEC = S_("GDEC", [P, 4 * 65], BF16)
    DMASK = S_("DMASK", [P, H * P], BF16)
    GQF = S_("GQF", [P, FH], BF16)
    KGM = S_("KGM", [P, FH], BF16)
    identb = S_("identb_sb", [P, P], BF16)
    tsc = G[:, 0:P]
    zmask = G[:, P:2 * P]
    r1t = G[:, 2 * P:3 * P]
    r127 = S_("r127_sb", [P, 1], F32)
    sel = S_("sel_sb", [H, FH], BF16)
    onesr = S_("onesr_sb", [1, P], F32)
    onescb = S_("onescb_sb", [P, 1], BF16)
    bgh = S_("bgh", [P, 4], F32)
    dls = S_("dls", [1, H], F32)
    lgn = S_("lgn", [1, H], F32)
    NLGB = S_("NLGB", [P, H], F32)
    NLGF = S_("NLGF", [P, 4], F32)
    G128B = S_("G128B", [P, H], F32)
    KGC = S_("KGC", [P, H], F32)
    REC = S_("REC", [P, H], F32)
    RECB = S_("RECB", [P, H], BF16)
    stats = S_("stats", [P, 16], F32)
    CONSTC = S_("CONSTC", [P, 10], F32)
    RECT = SCRB[0:H, 0:P]
    ONE64B = SCRB[:, 0:65]

    PS_ = nc.alloc_psum_tensor
    pb = [PS_(f"pb{i}", [P, FH], F32) for i in range(7)]
    pbT = PS_("pbT", [P, 2 * FH], BF16)

    E_pe = Eng("pe", no_drain=True)
    E_v, E_s, E_sy, E_gp = Eng("v"), Eng("s"), Eng("sy"), Eng("gp")

    def mm(out_ap, lhsT_ap, rhs_ap, start, stop, reads=(), writes=(), inc=False, tp=None):
        def f(e):
            return e.matmul(out_ap, lhsT_ap, rhs_ap, start=start, stop=stop,
                            tile_position=tp)
        return E_pe.op(f, reads=reads, writes=writes, inc=inc)

    for i4 in range(4):
        E_gp.dma(wq[:, i4 * 2 * FH:(i4 + 1) * 2 * FH],
                 wq_d[:, i4 * 2 * FH:(i4 + 1) * 2 * FH], f"lq{i4}")
    for w_sb, w_dr, wkey in [(wk, wk_d, "lk"), (wv, wv_d, "lv"),
                             (wg, wg_d, "lg"), (wo, wo_d, "lo")]:
        E_gp.dma(w_sb[:], w_dr, wkey)
    E_gp.dma(fc[:], fc_d, "wt2")
    E_gp.dma(pj[:], pj_d, "wt3")

    for dst, src in [(identb[:], identb_d), (tsc, tsc_d), (zmask, zmask_d),
                     (r1t, r1_d), (r127[:], r127_d), (sel[:], sel_d),
                     (onesr[:], onesr_d), (onescb[:], onescb_d),
                     (bgh[:], bg_d), (dls[:], dl_d)]:
        n_cst = E_sy.dma(dst, src, "cst")

    E_v.wait(E_sy, n_cst, dma=True)
    E_s.wait(E_sy, n_cst, dma=True)

    c = E_s.op(lambda e: e.activation(lgn[:], dls[:], AF.Exp, scale=-1.0),
               reads=("dls",), writes=("lgn",), inc=True)
    E_v.wait(E_s, c)
    c = E_v.op(lambda e: e.tensor_scalar_add(out=lgn[:], in0=lgn[:], scalar1=1.0),
               reads=("lgn",), writes=("lgn",), inc=True)
    E_s.wait(E_v, c)
    c_ln = E_s.op(lambda e: e.activation(lgn[:], lgn[:], AF.Ln),
                  reads=("lgn",), writes=("lgn",), inc=True)
    E_v.op(lambda e: e.tensor_scalar_mul(out=bgh[:], in0=bgh[:], scalar1=0.5),
           reads=("bgh",), writes=("bgh",))
    E_v.op(lambda e: e.memset(CONSTC[:, 0:1], -1.0), writes=("CONSTC",))
    E_v.op(lambda e: e.memset(CONSTC[:, 1:2], -6.0), writes=("CONSTC",))
    E_v.op(lambda e: e.memset(CONSTC[:, 2:3], 0.25), writes=("CONSTC",))
    E_v.op(lambda e: e.memset(CONSTC[:, 3:4], -12.0), writes=("CONSTC",))
    E_v.op(lambda e: e.memset(CONSTC[:, 4:5], 0.0), writes=("CONSTC",))
    E_v.op(lambda e: e.memset(CONSTC[:, 5:6], -THS[0]), writes=("CONSTC",))
    E_v.op(lambda e: e.memset(CONSTC[:, 6:7], -THS[1]), writes=("CONSTC",))
    E_v.op(lambda e: e.memset(CONSTC[:, 7:8], -THS[2]), writes=("CONSTC",))
    E_v.op(lambda e: e.memset(CONSTC[:, 8:9], -THS[3]), writes=("CONSTC",))
    E_v.op(lambda e: e.memset(CONSTC[:, 9:10], -THS[4]), writes=("CONSTC",))
    E_v.op(lambda e: e.memset(ZP[:], 0.0), writes=("ZP",))
    E_v.op(lambda e: e.memset(ONE64B, 1.0), writes=("SCRB",))
    c_ssb0 = E_v.op(lambda e: e.memset(SSB[:], 0.0), writes=("SSB",), inc=True)

    E_pe.wait(E_s, c_ln)
    c = mm(pb[5][0:P, 0:H], onesr[:], lgn[:], True, True,
           reads=("onesr", "lgn"), writes=("pb5",), inc=True)
    E_v.wait(E_pe, c)
    c_nl = E_v.op(lambda e: e.tensor_scalar_mul(out=NLGB[:], in0=pb[5][0:P, 0:H],
                                                scalar1=-1.0),
                  reads=("pb5",), writes=("NLGB",), inc=True)
    E_s.wait(E_v, c_nl)
    c_g128 = E_s.op(lambda e: e.activation(G128B[:], NLGB[:], AF.Exp, scale=128.0),
                    reads=("NLGB",), writes=("G128B",), inc=True)
    for h in range(H):
        c_kgc = E_s.op(lambda e, h=h: e.activation(KGC[:, h:h + 1], r127[:], AF.Exp,
                                                   scale=NLGB[:, h:h + 1]),
                       reads=("r127", "NLGB"), writes=("KGC",), inc=(h == H - 1))
    for h in range(H):
        c_dm = E_s.op(lambda e, h=h: e.activation(DMASK[:, h * P:(h + 1) * P], tsc,
                                                  AF.Exp, scale=NLGB[:, h:h + 1]),
                      reads=("G", "NLGB"), writes=("DMASK",), inc=(h == H - 1))
    for ft in range(4):
        E_v.op(lambda e, ft=ft: e.tensor_copy(out=NLGF[0:DKH, ft:ft + 1],
                                              in_=NLGB[0:DKH, 2 * ft:2 * ft + 1]),
               reads=("NLGB",), writes=("NLGF",))
        c_nlgf = E_v.op(lambda e, ft=ft: e.tensor_copy(
            out=NLGF[DKH:P, ft:ft + 1], in_=NLGB[DKH:P, 2 * ft + 1:2 * ft + 2]),
            reads=("NLGB",), writes=("NLGF",), inc=(ft == 3))
    E_s.wait(E_v, c_nlgf)
    for ft in range(4):
        c_gqf = E_s.op(lambda e, ft=ft: e.activation(GQF[:, ft * P:(ft + 1) * P], r1t,
                                                     AF.Exp, scale=NLGF[:, ft:ft + 1]),
                       reads=("G", "NLGF"), writes=("GQF",), inc=(ft == 3))
    E_v.wait(E_s, c_dm)
    for h in range(H):
        E_v.op(lambda e, h=h: e.tensor_tensor(out=DMASK[:, h * P:(h + 1) * P],
                                              in0=DMASK[:, h * P:(h + 1) * P],
                                              in1=zmask, op=ALU.mult),
               reads=("DMASK", "G"), writes=("DMASK",))
    E_v.wait(E_s, c_kgc)
    for h in range(H):
        E_v.op(lambda e, h=h: e.tensor_scalar_mul(out=KGM[:, h * DKH:(h + 1) * DKH],
                                                  in0=ONE64B[:, 0:DKH],
                                                  scalar1=KGC[:, h:h + 1]),
               reads=("SCRB", "KGC"), writes=("KGM",))
        E_v.op(lambda e, h=h: e.tensor_copy(out=VAUG[:, h * 65 + 64:h * 65 + 65],
                                            in_=KGC[:, h:h + 1]),
               reads=("KGC",), writes=("VAUG",))
    E_v.wait(E_s, c_g128)
    for p_ in range(4):
        for r_ in range(2):
            h = 2 * p_ + r_
            E_v.op(lambda e, p_=p_, r_=r_, h=h: e.tensor_scalar_mul(
                out=GDEC[r_ * DKH:(r_ + 1) * DKH, p_ * 65:(p_ + 1) * 65],
                in0=ONE64B[r_ * DKH:(r_ + 1) * DKH, 0:65],
                scalar1=G128B[r_ * DKH:(r_ + 1) * DKH, h:h + 1]),
                reads=("SCRB", "G128B"), writes=("GDEC",), inc=(p_ == 3 and r_ == 1))

    for i4 in range(4):
        E_pe.wait(E_gp, (f"lq{i4}", 1), dma=True)
    E_pe.wait(E_v, c_ssb0)

    xt_cnt = {}
    xnact_done = {}
    xnte_done = {}
    xntp_done = {}
    xnstat_done = {}
    gp_guard = {"s": 0, "g": 0}

    def load_xt(ch):
        E_gp.wait(E_s, xnact_done.get(ch - 2, 0))
        xt_cnt[ch] = E_gp.dma(XT[ch % 2][:], x_tok_d[ch], f"xt{ch % 2}")

    def emit_A_stats(ch):
        buf = XT[ch % 2]
        key = f"XT{ch % 2}"
        st_ = stats
        rcol = 8 if ch % 2 == 0 else 14
        ncol = 9 if ch % 2 == 0 else 15
        E_gp.wait(E_gp, xt_cnt[ch], dma=True)
        E_s.wait(E_gp, xt_cnt[ch], dma=True)
        E_s.wait(E_gp, gp_guard["g"])
        E_s.op(lambda e, b=buf: e.activation(XF[:], b[:], AF.Identity,
                                             accum_out=st_[:, 0:1]),
               reads=(key,), writes=("XF",))
        E_s.op(lambda e, b=buf: e.activation(XF[:, 0:FH], b[:, 0:FH], AF.Square,
                                             accum_out=st_[:, 1:2]),
               reads=(key,), writes=("XF",))
        c_sq = E_s.op(lambda e, b=buf: e.activation(XF[:, 0:FH], b[:, FH:D], AF.Square,
                                                    accum_out=st_[:, 2:3]),
                      reads=(key,), writes=("XF",), inc=True)
        E_gp.wait(E_s, c_sq)
        E_gp.wait(E_s, gp_guard["s"])
        g = E_gp
        g.op(lambda e: e.tensor_scalar_mul(out=st_[:, 12:13], in0=st_[:, 0:1],
                                           scalar1=1.0 / D),
             reads=("stats",), writes=("stats",))
        g.op(lambda e: e.tensor_scalar_mul(out=st_[:, 13:14], in0=st_[:, 0:1],
                                           scalar1=-1.0 / D),
             reads=("stats",), writes=("stats",))
        g.op(lambda e: e.tensor_tensor(out=st_[:, 3:4], in0=st_[:, 1:2],
                                       in1=st_[:, 2:3], op=ALU.add),
             reads=("stats",), writes=("stats",))
        g.op(lambda e: e.tensor_scalar(out=st_[:, 3:4], in0=st_[:, 3:4],
                                       scalar1=1.0 / D, scalar2=EPS_LN,
                                       op0=ALU.mult, op1=ALU.add),
             reads=("stats",), writes=("stats",))
        g.op(lambda e: e.tensor_tensor(out=st_[:, 4:5], in0=st_[:, 12:13],
                                       in1=st_[:, 13:14], op=ALU.mult),
             reads=("stats",), writes=("stats",))
        g.op(lambda e: e.tensor_tensor(out=st_[:, 5:6], in0=st_[:, 4:5],
                                       in1=st_[:, 3:4], op=ALU.add),
             reads=("stats",), writes=("stats",))
        g.op(lambda e: e.tensor_scalar(out=st_[:, 6:7], in0=st_[:, 5:6],
                                       scalar1=-0.5, scalar2=1.5,
                                       op0=ALU.mult, op1=ALU.add),
             reads=("stats",), writes=("stats",))
        for _ in range(3):
            g.op(lambda e: e.tensor_tensor(out=st_[:, 7:8], in0=st_[:, 6:7],
                                           in1=st_[:, 6:7], op=ALU.mult),
                 reads=("stats",), writes=("stats",))
            g.op(lambda e: e.tensor_tensor(out=st_[:, 7:8], in0=st_[:, 7:8],
                                           in1=st_[:, 5:6], op=ALU.mult),
                 reads=("stats",), writes=("stats",))
            g.op(lambda e: e.tensor_scalar(out=st_[:, 7:8], in0=st_[:, 7:8],
                                           scalar1=-0.5, scalar2=1.5,
                                           op0=ALU.mult, op1=ALU.add),
                 reads=("stats",), writes=("stats",))
            g.op(lambda e: e.tensor_tensor(out=st_[:, 6:7], in0=st_[:, 6:7],
                                           in1=st_[:, 7:8], op=ALU.mult),
                 reads=("stats",), writes=("stats",))
        g.op(lambda e: e.tensor_scalar_mul(out=st_[:, rcol:rcol + 1],
                                           in0=st_[:, 6:7], scalar1=1.0),
             reads=("stats",), writes=("stats",))
        g.op(lambda e: e.tensor_tensor(out=st_[:, 10:11], in0=st_[:, 12:13],
                                       in1=st_[:, rcol:rcol + 1], op=ALU.mult),
             reads=("stats",), writes=("stats",))
        c_neg = g.op(lambda e: e.tensor_scalar_mul(out=st_[:, ncol:ncol + 1],
                                                   in0=st_[:, 10:11], scalar1=-1.0),
                     reads=("stats",), writes=("stats",), inc=True)
        gp_guard["g"] = c_neg
        xnstat_done[ch] = c_neg

    def emit_A_xn(ch, cc):
        buf = XT[ch % 2]
        key = f"XT{ch % 2}"
        st_ = stats
        rcol = 8 if ch % 2 == 0 else 14
        ncol = 9 if ch % 2 == 0 else 15
        E_s.wait(E_gp, xnstat_done[ch])
        E_s.wait(E_pe, gp_guard.get("tp", 0))
        c_xn = E_s.op(lambda e, b=buf, rcol=rcol, ncol=ncol: e.activation(
            XF[:], b[:], AF.Identity, scale=st_[:, rcol:rcol + 1],
            bias=st_[:, ncol:ncol + 1]),
            reads=(key, "stats"), writes=("XF",), inc=True)
        xnact_done[ch] = c_xn
        gp_guard["s"] = c_xn
        E_pe.wait(E_s, c_xn)
        E_pe.wait(E_v, gp_guard.get("te", 0))
        for dk in range(8):
            c_T = E_pe.op(lambda e, dk=dk: e.transpose(pbT[:, dk * P:(dk + 1) * P],
                                                       XF[:, dk * P:(dk + 1) * P],
                                                       identb[:]),
                          reads=("XF", "identb"), writes=("pbT",), inc=(dk == 7))
        xntp_done[ch] = c_T
        gp_guard["tp"] = c_T
        E_v.wait(E_pe, c_T)
        xr3 = XR[:].rearrange("p (a b) -> p a b", b=ST * CH)[:, :, cc * CH:(cc + 1) * CH]
        pt3 = pbT[:, 0:8 * P].rearrange("p (a b) -> p a b", b=P)
        c_ev = E_v.op(lambda e, xr3=xr3, pt3=pt3: e.tensor_copy(
            out=xr3, in_=pt3),
            reads=("pbT",), writes=("XR",), inc=True)
        xnte_done[ch] = c_ev
        gp_guard["te"] = c_ev

    load_xt(0)
    load_xt(1)
    emit_A_stats(0)
    emit_A_stats(1)
    emit_A_xn(0, 0)
    emit_A_xn(1, 1)

    combine_done = {-1: 0, -2: 0}
    combine_mul = {-1: 0, -2: 0}
    store_cnt = {-1: ("st", 0)}
    c_vtab = E_v.count
    th_done = {}
    vaug_done = {}
    ssb_done = {-1: c_ssb0}
    yb_done = {-1: c_vtab, -2: 0}
    wo_done = {-1: 0}
    ptail = {0: (0, 0, 0), 1: (0, 0, 0)}
    yq_free_v = 0
    yq_free_s = 0
    yev0 = 0
    yev1 = 0
    y2ev = {}
    pending_c1 = None

    for sti in range(nst):
        c0 = sti * ST
        chunks = [c0, c0 + 1]
        for cc, ch in enumerate(chunks):
            qbank, kbank = pb[2], pb[3]
            if sti == 0 and cc == 0:
                for i4 in range(4):
                    E_pe.wait(E_gp, (f"lq{i4}", 1), dma=True)
            if cc == 0:
                E_pe.wait(E_v, combine_mul[c0 - 2])
                E_pe.wait(E_v, xnte_done.get(ch, 0))
            else:
                E_pe.wait(E_s, th_done[ch - 1])
                E_pe.wait(E_v, vaug_done[ch - 1])
                E_pe.wait(E_v, xnte_done.get(ch, 0))
            for ft in range(4):
                for dk in range(8):
                    c_qps = mm(qbank[:, ft * P:(ft + 1) * P],
                               wq[:, dk * FH + ft * P:dk * FH + (ft + 1) * P],
                               XR[:, dk * ST * CH + cc * CH:dk * ST * CH + cc * CH + CH],
                               start=(dk == 0), stop=(dk == 7),
                               reads=("wq_sb", "XR"), writes=("pb2",),
                               inc=(ft == 3 and dk == 7))
            E_s.wait(E_pe, c_qps)
            E_s.wait(E_pe, wo_done.get(ch - 1, 0))
            E_s.wait(E_v, yb_done.get(ch - 1, 0))
            if cc == 0:
                E_s.wait(E_v, yq_free_v)
            E_s.op(lambda e: e.activation(QF[:], qbank[:, 0:FH], AF.Relu, scale=-1.0),
                   reads=("pb2",), writes=("QF",))
            c_qph = E_s.op(lambda e: e.activation(SCRB[:], QF[:], AF.Exp, scale=-1.0),
                           reads=("QF",), writes=("SCRB",), inc=True)
            E_v.wait(E_pe, c_qps)
            E_v.wait(E_s, c_qph)
            c_qf = E_v.op(lambda e: e.scalar_tensor_tensor(
                out=QF[:], in0=qbank[:, 0:FH], scalar=0.0, in1=SCRB[:],
                op0=ALU.max, op1=ALU.add),
                reads=("pb2", "SCRB"), writes=("QF",), inc=True)
            if sti == 0 and cc == 0:
                E_pe.wait(E_gp, ("lk", 1), dma=True)
            for ft in range(4):
                for dk in range(8):
                    c_kps = mm(kbank[:, ft * P:(ft + 1) * P],
                               wk[:, dk * FH + ft * P:dk * FH + (ft + 1) * P],
                               XR[:, dk * ST * CH + cc * CH:dk * ST * CH + cc * CH + CH],
                               start=(dk == 0), stop=(dk == 7),
                               reads=("wk_sb", "XR"), writes=("pb3",),
                               inc=(ft == 3 and dk == 7))
            E_s.wait(E_pe, c_kps)
            E_s.wait(E_v, c_qf)
            E_s.op(lambda e: e.activation(KF[:], kbank[:, 0:FH], AF.Relu, scale=-1.0),
                   reads=("pb3",), writes=("KF",))
            c_kph = E_s.op(lambda e: e.activation(SCRB[:], KF[:], AF.Exp, scale=-1.0),
                           reads=("KF",), writes=("SCRB",), inc=True)
            E_v.wait(E_pe, c_kps)
            E_v.wait(E_s, c_kph)
            c_kf = E_v.op(lambda e: e.scalar_tensor_tensor(
                out=KF[:], in0=kbank[:, 0:FH], scalar=0.0, in1=SCRB[:],
                op0=ALU.max, op1=ALU.add),
                reads=("pb3", "SCRB"), writes=("KF",), inc=True)
            E_v.op(lambda e: e.tensor_copy(out=ZP[0:DKH, 0:FH], in_=KF[0:DKH, 0:FH]),
                   reads=("KF",), writes=("ZP",))
            c_kz = E_v.op(lambda e: e.tensor_copy(out=ZP[DKH:P, FH:2 * FH],
                                                  in_=KF[DKH:P, 0:FH]),
                          reads=("KF",), writes=("ZP",), inc=True)
            if cc == 0 and pending_c1 is not None:
                pending_c1()
                gp_guard["g"] = combine_done[c0 - 1]
                pending_c1 = None
            if sti == 0 and cc == 0:
                E_pe.wait(E_gp, ("lg", 1), dma=True)
            E_pe.wait(E_s, c_qph)
            E_pe.wait(E_v, c_qf)
            for ft in range(4):
                for dk in range(8):
                    c_gps = mm(qbank[:, ft * P:(ft + 1) * P],
                               wg[:, dk * FH + ft * P:dk * FH + (ft + 1) * P],
                               XR[:, dk * ST * CH + cc * CH:dk * ST * CH + cc * CH + CH],
                               start=(dk == 0), stop=(dk == 7),
                               reads=("wg_sb", "XR"), writes=("pb2",),
                               inc=(ft == 3 and dk == 7))
            E_s.wait(E_pe, c_gps)
            E_s.wait(E_v, yb_done.get(ch - 1, 0))
            for ft in range(4):
                c_th = E_s.op(lambda e, ft=ft: e.activation(
                    THF[:, ft * P:(ft + 1) * P], qbank[:, ft * P:(ft + 1) * P],
                    AF.Tanh, scale=0.5, bias=bgh[:, ft:ft + 1]),
                    reads=("pb2", "bgh"), writes=("THF",), inc=(ft == 3))
            th_done[ch] = c_th
            if sti == 0 and cc == 0:
                E_pe.wait(E_gp, ("lv", 1), dma=True)
            E_pe.wait(E_s, c_kph)
            E_pe.wait(E_v, c_kf)
            for dk in range(8):
                c_vps = mm(kbank[:, 0:FH],
                           XR[:, dk * ST * CH + cc * CH:dk * ST * CH + cc * CH + CH],
                           wv[:, dk * FH:(dk + 1) * FH],
                           start=(dk == 0), stop=(dk == 7),
                           reads=("wv_sb", "XR"), writes=("pb3",), inc=(dk == 7))
            E_v.wait(E_pe, c_vps)
            E_v.op(lambda e: e.tensor_copy(out=VT[:], in_=kbank[:, 0:FH]),
                   reads=("pb3",), writes=("VT",))
            vaug3 = VAUG[:].rearrange("p (h w) -> p h w", w=65)[:, :, 0:DKH]
            vt3 = VT[:].rearrange("p (h w) -> p h w", w=DKH)
            kg3 = KGM[:].rearrange("p (h w) -> p h w", w=DKH)
            c_vaug = E_v.op(lambda e: e.tensor_tensor(out=vaug3, in0=vt3, in1=kg3,
                                                      op=ALU.mult),
                            reads=("VT", "KGM"), writes=("VAUG",), inc=True)
            vaug_done[ch] = c_vaug
            E_pe.wait(E_v, c_kz)
            E_pe.wait(E_v, ssb_done.get(ch - 1, 0))
            if cc == 0:
                E_pe.wait(E_s, yev0)
                E_pe.wait(E_v, yev1)
                E_pe.wait(E_v, combine_mul[c0 - 1])
            for h in range(H):
                r_, ft = h % 2, h // 2
                abank = pb[h // 4]
                c_amm = mm(abank[:, (h % 4) * P:(h % 4 + 1) * P],
                           ZP[:, r_ * FH + ft * P:r_ * FH + (ft + 1) * P],
                           QF[:, ft * P:(ft + 1) * P],
                           True, True, reads=("ZP", "QF"), writes=(f"pb{h // 4}",),
                           inc=(h == H - 1))
            E_v.wait(E_pe, c_amm)
            E_v.op(lambda e: e.tensor_tensor(out=AT[:, 0:FH], in0=pb[0][:, 0:FH],
                                             in1=DMASK[:, 0:FH], op=ALU.mult),
                   reads=("pb0", "DMASK"), writes=("AT",))
            c_at = E_v.op(lambda e: e.tensor_tensor(out=AT[:, FH:2 * FH],
                                                    in0=pb[1][:, 0:FH],
                                                    in1=DMASK[:, FH:2 * FH],
                                                    op=ALU.mult),
                          reads=("pb1", "DMASK"), writes=("AT",), inc=True)
            E_v.op(lambda e: e.tensor_tensor(out=QF[:], in0=QF[:], in1=GQF[:],
                                             op=ALU.mult),
                   reads=("QF", "GQF"), writes=("QF",))
            E_v.op(lambda e: e.tensor_copy(out=ZP[0:DKH, 0:FH], in_=QF[0:DKH, 0:FH]),
                   reads=("QF",), writes=("ZP",))
            c_qz = E_v.op(lambda e: e.tensor_copy(out=ZP[DKH:P, FH:2 * FH],
                                                  in_=QF[DKH:P, 0:FH]),
                          reads=("QF",), writes=("ZP",), inc=True)
            E_pe.wait(E_v, c_at)
            if cc == 0:
                E_pe.wait(E_v, xnte_done.get(ch + 1, 0))
            for ft in range(4):
                c_kt = E_pe.op(lambda e, ft=ft: e.transpose(
                    pbT[:, ft * P:(ft + 1) * P], KF[:, ft * P:(ft + 1) * P],
                    identb[:]),
                    reads=("KF", "identb"), writes=("pbT",), inc=(ft == 3))
            opb = pb[4] if cc == 0 else pb[5]
            opkey = "pb4" if cc == 0 else "pb5"
            E_pe.wait(E_v, c_qz)
            if cc == 0:
                E_pe.wait(E_s, yq_free_s)
                E_pe.wait(E_v, yq_free_v)
            for p_ in range(4):
                for r_ in range(2):
                    h = 2 * p_ + r_
                    mm(opb[r_ * DKH:(r_ + 1) * DKH, p_ * P:(p_ + 1) * P],
                       SSB[:, p_ * 65:p_ * 65 + DKH],
                       ZP[:, r_ * FH + p_ * P:r_ * FH + (p_ + 1) * P],
                       True, False, reads=("SSB", "ZP"), writes=(opkey,),
                       tp=(0, r_ * DKH))
                    mm(opb[r_ * DKH:(r_ + 1) * DKH, p_ * P:(p_ + 1) * P],
                       VT[:, h * DKH:(h + 1) * DKH],
                       AT[:, h * P:(h + 1) * P],
                       False, True, reads=("VT", "AT"), writes=(opkey,),
                       tp=(0, r_ * DKH))
            for h in range(H):
                p_, r_ = h // 2, h % 2
                mm(pb[0][0:P, h:h + 1], AT[:, h * P:(h + 1) * P], onescb[:],
                   True, False, reads=("AT", "onescb"), writes=("pb0",))
                c_omm = mm(pb[0][0:P, h:h + 1],
                           ZP[:, r_ * FH + p_ * P:r_ * FH + (p_ + 1) * P],
                           SSB[:, p_ * 65 + DKH:p_ * 65 + 65],
                           False, True, reads=("ZP", "SSB"), writes=("pb0",),
                           inc=(h == H - 1))
            E_v.wait(E_pe, c_kt)
            c_ktok = E_v.op(lambda e: e.tensor_copy(out=KF[:], in_=pbT[:, 0:FH]),
                            reads=("pbT",), writes=("KF",), inc=True)
            E_v.wait(E_pe, c_omm)
            E_v.op(lambda e: e.tensor_scalar_add(out=REC[:], in0=pb[0][0:P, 0:H],
                                                 scalar1=EPS_AT),
                   reads=("pb0",), writes=("REC",))
            E_v.op(lambda e: e.reciprocal(out=REC[:], in_=REC[:]),
                   reads=("REC",), writes=("REC",))
            c_rec = E_v.op(lambda e: e.tensor_copy(out=RECB[:], in_=REC[:]),
                           reads=("REC",), writes=("RECB",), inc=True)
            E_pe.wait(E_v, c_rec)
            c_rt = E_pe.op(lambda e: e.transpose(pbT[0:H, FH:FH + P], RECB[:],
                                                 identb[:]),
                           reads=("RECB", "identb"), writes=("pbT",), inc=True)
            E_v.wait(E_pe, c_rt)
            c_rte = E_v.op(lambda e: e.tensor_copy(out=RECT, in_=pbT[0:H, FH:FH + P]),
                           reads=("pbT",), writes=("SCRB",), inc=True)
            E_pe.wait(E_v, c_rte)
            for ft in range(4):
                mm(pb[1][:, ft * P:(ft + 1) * P], sel[:, ft * P:(ft + 1) * P],
                   RECT, True, True, reads=("sel", "SCRB"), writes=("pb1",))
            for p_ in range(4):
                for r_ in range(2):
                    h = 2 * p_ + r_
                    ft = h // 2
                    c_stmm = mm(pb[0][r_ * DKH:(r_ + 1) * DKH,
                                      128 + p_ * 65:128 + (p_ + 1) * 65],
                                KF[:, ft * P + r_ * DKH:ft * P + (r_ + 1) * DKH],
                                VAUG[:, h * 65:(h + 1) * 65],
                                True, True, reads=("KF", "VAUG"), writes=("pb0",),
                                tp=(0, r_ * DKH), inc=(h == H - 1))
            ybbuf = SGN if cc == 0 else SCRB
            ybkey = "SGN" if cc == 0 else "SCRB"
            E_v.wait(E_pe, c_stmm)
            E_v.wait(E_s, c_th)
            if cc == 0:
                E_v.wait(E_pe, wo_done.get(ch - 2, 0))
            E_v.op(lambda e, b=opb: e.scalar_tensor_tensor(
                out=QF[:], in0=THF[:], scalar=1.0, in1=b[:, 0:FH],
                op0=ALU.add, op1=ALU.mult),
                reads=("THF", opkey), writes=("QF",))
            c_yb = E_v.op(lambda e, yb_=ybbuf: e.scalar_tensor_tensor(
                out=yb_[:, 0:FH], in0=QF[:], scalar=0.5, in1=pb[1][:, 0:FH],
                op0=ALU.mult, op1=ALU.mult),
                reads=("QF", "pb1"), writes=(ybkey,), inc=True)
            yb_done[ch] = c_yb
            E_v.op(lambda e: e.tensor_tensor(out=SSB[:], in0=SSB[:], in1=GDEC[:],
                                             op=ALU.mult),
                   reads=("SSB", "GDEC"), writes=("SSB",))
            c_ssb = E_v.op(lambda e: e.tensor_tensor(out=SSB[:], in0=SSB[:],
                                                     in1=pb[0][:, 128:388], op=ALU.add),
                           reads=("SSB", "pb0"), writes=("SSB",), inc=True)
            ssb_done[ch] = c_ssb
            if sti + 1 < nst:
                load_xt(c0 + 2 + cc)
                emit_A_stats(c0 + 2 + cc)

        relu_cnt = {}
        h2_cnt = {}
        fc_cnt = {}
        pj_cnt = {}
        ymap = {(0, 0): 4, (0, 1): 5, (1, 0): 0, (1, 1): 1}
        th_c1 = c_th
        vaug_c1 = c_vaug

        def emit_fc(ft):
            bank = pb[2 + ft % 2]
            E_pe.wait(E_v, h2_cnt.get(ft - 2, 0))
            if ft == 0:
                E_pe.wait(E_s, th_c1)
                E_pe.wait(E_v, vaug_c1)
            for dk in range(8):
                c_fc = mm(bank[:, 0:ST * CH],
                          fc[:, dk * FF + ft * P:dk * FF + (ft + 1) * P],
                          XR[:, dk * ST * CH:(dk + 1) * ST * CH],
                          start=(dk == 0), stop=(dk == 7),
                          reads=("fc_sb", "XR"), writes=(f"pb{2 + ft % 2}",),
                          inc=(dk == 7))
            fc_cnt[ft] = c_fc
            E_s.wait(E_pe, c_fc)
            E_s.wait(E_pe, pj_cnt.get(ft - 2, 0))
            relu_cnt[ft] = E_s.op(
                lambda e, b=bank, ft=ft: e.activation(H2[ft % 2][:], b[:, 0:ST * CH],
                                                      AF.Relu),
                reads=(f"pb{2 + ft % 2}",), writes=(f"H2_{ft % 2}",), inc=True)
            E_v.wait(E_s, relu_cnt[ft])
            h2_cnt[ft] = E_v.op(
                lambda e, b=bank, ft=ft: e.tensor_tensor(
                    out=H2[ft % 2][:], in0=H2[ft % 2][:], in1=b[:, 0:ST * CH],
                    op=ALU.mult),
                reads=(f"H2_{ft % 2}", f"pb{2 + ft % 2}"),
                writes=(f"H2_{ft % 2}",), inc=True)

        def emit_pj(ft):
            E_pe.wait(E_v, h2_cnt[ft])
            if ft == 0:
                E_pe.wait(E_v, ssb_done[c0 + 1])
            for tsub in range(ST):
                for doh in range(2):
                    bank = pb[ymap[(tsub, doh)]]
                    c_pj = mm(bank[:, 0:FH],
                              H2[ft % 2][:, tsub * CH:(tsub + 1) * CH],
                              pj[:, ft * D + doh * FH:ft * D + (doh + 1) * FH],
                              start=(ft == 0), stop=(ft == 31),
                              reads=(f"H2_{ft % 2}", "pj_sb"),
                              writes=(f"pb{ymap[(tsub, doh)]}",),
                              inc=(tsub == ST - 1 and doh == 1))
            pj_cnt[ft] = c_pj

        if sti == 0:
            E_pe.wait(E_gp, ("wt2", 1), dma=True)
        emit_fc(0)
        emit_fc(1)
        if sti == 0:
            E_pe.wait(E_gp, ("wt3", 1), dma=True)
        for ft in range(1, 32):
            emit_pj(ft - 1)
            if ft + 1 < 32:
                emit_fc(ft + 1)
        emit_pj(31)
        c_yready = pj_cnt[31]
        h2_31 = h2_cnt[31]

        if sti == 0:
            E_pe.wait(E_gp, ("lo", 1), dma=True)
        E_pe.wait(E_v, yb_done[c0])
        E_pe.wait(E_v, h2_31)
        for doh in range(2):
            for p_ in range(4):
                c_wo0 = mm(pb[2 + doh][:, 0:FH], SGN[:, p_ * P:(p_ + 1) * P],
                           wo[:, p_ * D + doh * FH:p_ * D + (doh + 1) * FH],
                           start=(p_ == 0), stop=(p_ == 3),
                           reads=("SGN", "wo_sb"), writes=(f"pb{2 + doh}",),
                           inc=(doh == 1 and p_ == 3))
        wo_done[c0] = c_wo0

        inv = {v: k for k, v in ymap.items()}
        ybanks = [4, 5, 0, 1]
        E_v.wait(E_pe, c_yready)
        E_v.wait(E_sy, store_cnt.get(c0 - 1, ("st", 0)), dma=True)
        E_v.wait(E_gp, combine_done.get(c0 - 1, 0))
        E_s.wait(E_pe, wo_done[c0])

        def spike_group(piE, piO):
            ctx = {}
            for par, pi in ((0, piE), (1, piO)):
                bk = ybanks[pi]
                tsub, doh = inv[bk]
                ctx[par] = dict(
                    bk=bk, gc0=tsub * D + doh * FH,
                    MB=MEM[:, par * FH:(par + 1) * FH],
                    sgs=((SGN, "SGN"), (QF, "QF")) if par == 0 else
                        ((SGN2, "SGN2"), (KF, "KF")),
                    accb=pb[6] if par == 0 else pb[4],
                    acck="pb6" if par == 0 else "pb4")
            cE, cO = ctx[0], ctx[1]
            if piE == 0:
                ysrcE, ysrcEk = MEM[:, 0:FH], "MEM"
                c_yev = E_v.op(lambda e: e.tensor_scalar_mul(
                    out=MEM[:, 0:FH], in0=pb[4][:, 0:FH], scalar1=1.0),
                    reads=("pb4",), writes=("MEM",), inc=True)
            else:
                ysrcE, ysrcEk = pb[cE["bk"]][:, 0:FH], f"pb{cE['bk']}"
                c_yev = 0
            aE = {-2: ptail[0][2], -1: ptail[0][2]}
            aO = {-2: ptail[1][2], -1: ptail[1][2]}
            for j in range(5):
                bufE, keyE = cE["sgs"][j % 2]
                bufO, keyO = cO["sgs"][j % 2]
                E_s.wait(E_pe, aE[j - 2])
                if j == 0:
                    E_s.wait(E_v, ptail[0][0])
                    if piE == 0:
                        E_s.wait(E_v, c_yev)
                sc = E_s.op(lambda e, bufE=bufE, j=j, y_=ysrcE: e.activation(
                    bufE[:, 0:FH], y_, AF.Sign,
                    bias=CONSTC[:, 5 + j:6 + j]),
                    reads=(ysrcEk, "CONSTC"), writes=(keyE,), inc=True)
                E_v.wait(E_pe, aO[j - 2])
                vc = E_v.op(lambda e, bufO=bufO, j=j, c=cO: e.tensor_scalar(
                    out=bufO[:, 0:FH], in0=pb[c["bk"]][:, 0:FH],
                    scalar1=THS[j], scalar2=2.0, op0=ALU.is_gt, op1=ALU.mult),
                    reads=(f"pb{cO['bk']}",), writes=(keyO,), inc=True)
                E_pe.wait(E_s, sc)
                if j == 0:
                    E_pe.wait(E_s, ptail[0][1])
                    E_pe.wait(E_v, ptail[0][0])
                aE[j] = mm(cE["accb"][:, 0:FH], identb[:], bufE[:, 0:FH],
                           j == 0, j == 4, reads=("identb", keyE),
                           writes=("pb6",), inc=True)
                E_pe.wait(E_v, vc)
                if j == 0:
                    E_pe.wait(E_s, ptail[1][1])
                    E_pe.wait(E_v, ptail[1][0])
                    if piE == 0:
                        E_pe.wait(E_v, c_yev)
                aO[j] = mm(cO["accb"][:, 0:FH], identb[:], bufO[:, 0:FH],
                           j == 0, j == 4, reads=("identb", keyO),
                           writes=("pb4",), inc=True)
            y2ev[cE["bk"]] = ("s", sc)
            y2ev[cO["bk"]] = ("v", vc)
            for par in (0, 1):
                c = ctx[par]
                acc4 = aE[4] if par == 0 else aO[4]
                zb = CONSTC[:, 2:3] if par == 0 else CONSTC[:, 4:5]
                zoff = 0.25 if par == 0 else 0.0
                sgb, sgk = c["sgs"][0]
                E_s.wait(E_pe, acc4)
                thg = E_s.op(lambda e, c=c, sgb=sgb, zb=zb: e.activation(
                    sgb[:, 0:FH], c["accb"][:, 0:FH], AF.Tanh, scale=0.05,
                    bias=zb),
                    reads=(c["acck"], "CONSTC"), writes=(sgk,), inc=True)
                E_v.wait(E_s, thg)
                E_v.op(lambda e, c=c, zoff=zoff: e.tensor_scalar(
                    out=c["MB"], in0=c["accb"][:, 0:FH], scalar1=0.05,
                    scalar2=zoff, op0=ALU.mult, op1=ALU.add),
                    reads=(c["acck"],), writes=("MEM",))
                zav = E_v.op(lambda e, c=c, sgb=sgb: e.scalar_tensor_tensor(
                    out=G[:, c["gc0"]:c["gc0"] + FH], in0=sgb[:, 0:FH],
                    scalar=1.0, in1=c["MB"], op0=ALU.add, op1=ALU.mult),
                    reads=(sgk, "MEM"), writes=("G",), inc=True)
                ptail[par] = (zav, thg, acc4)

        def emit_combine(cc, ch, xguard, s_tail, g_tail):
            if xguard:
                E_gp.wait(E_pe, xguard)
            E_gp.wait(E_gp, combine_done.get(ch - 1, 0))
            xc = E_gp.dma(XF[:], x_tok_d[ch], "xc")
            E_v.wait(E_pe, wo_done[ch])
            E_v.wait(E_sy, store_cnt.get(ch - 1, ("st", 0)), dma=True)
            E_v.wait(E_s, s_tail)
            E_v.wait(E_gp, g_tail)
            sbanks = ((pb[2], "pb2"), (pb[3], "pb3")) if cc == 0 else \
                     ((pb[5], "pb5"), (pb[0], "pb0"))
            for doh in range(2):
                sb_, sbk = sbanks[doh]
                c_mul = E_v.op(lambda e, doh=doh, cc=cc, sb_=sb_: e.tensor_tensor(
                    out=OUTB[:, doh * FH:(doh + 1) * FH], in0=sb_[:, 0:FH],
                    in1=G[:, cc * D + doh * FH:cc * D + (doh + 1) * FH],
                    op=ALU.mult),
                    reads=(sbk, "G"), writes=("MEM",), inc=(doh == 1))
            combine_mul[ch] = c_mul
            E_gp.wait(E_gp, xc, dma=True)
            E_gp.wait(E_v, c_mul)
            for doh in range(2):
                c_comb = E_gp.op(lambda e, doh=doh: e.tensor_tensor(
                    out=OUTB[:, doh * FH:(doh + 1) * FH],
                    in0=OUTB[:, doh * FH:(doh + 1) * FH],
                    in1=XF[:, doh * FH:(doh + 1) * FH], op=ALU.add),
                    reads=("MEM", "XF"), writes=("MEM",), inc=(doh == 1))
            combine_done[ch] = c_comb
            E_sy.wait(E_gp, c_comb)
            store_cnt[ch] = E_sy.dma(out_d[ch], OUTB[:, 0:D], "st")

        def wo1_doh0():
            E_pe.wait(E_v, y2ev[5][1])
            E_pe.wait(E_v, yb_done[c0 + 1])
            for p_ in range(4):
                mm(pb[5][:, 0:FH], SCRB[:, p_ * P:(p_ + 1) * P],
                   wo[:, p_ * D:p_ * D + FH], start=(p_ == 0), stop=(p_ == 3),
                   reads=("SCRB", "wo_sb"), writes=("pb5",))

        def wo1_doh1():
            E_pe.wait(E_s, y2ev[0][1])
            for p_ in range(4):
                c_wo1 = mm(pb[0][:, 0:FH], SCRB[:, p_ * P:(p_ + 1) * P],
                           wo[:, p_ * D + FH:(p_ + 1) * D], start=(p_ == 0),
                           stop=(p_ == 3), reads=("SCRB", "wo_sb"),
                           writes=("pb0",), inc=(p_ == 3))
            wo_done[c0 + 1] = c_wo1

        spike_group(0, 1)
        wo1_doh0()
        if sti + 1 < nst:
            E_pe.wait(E_v, ssb_done[c0 + 1])
            E_pe.wait(E_v, yb_done[c0 + 1])
            emit_A_xn(c0 + 2, 0)
        spike_group(2, 3)
        wo1_doh1()
        if sti + 1 < nst:
            emit_A_xn(c0 + 3, 1)
        yq_free_v = ptail[1][0]
        yq_free_s = ptail[1][1]
        yev0 = y2ev[0][1]
        yev1 = y2ev[1][1]
        spk_s_tail = ptail[1][1]
        spk_g_tail = E_gp.count
        xg_st = xntp_done[c0 + 3] if sti + 1 < nst else 0
        emit_combine(0, c0, xg_st, spk_s_tail, spk_g_tail)
        pending_c1 = (lambda ec=emit_combine, ch=c0 + 1, xg=xg_st,
                      st_=spk_s_tail, gt=spk_g_tail: ec(1, ch, xg, st_, gt))

    if pending_c1 is not None:
        pending_c1()
        pending_c1 = None
    E_sy.wait(E_sy, store_cnt[nch - 1], dma=True)

    for E in (E_pe, E_v, E_s, E_sy, E_gp):
        E.sem = nc.alloc_semaphore(f"sem_{E.name}")
        if hasattr(E, "dsem"):
            for key in E.dsem:
                E.dsem[key] = nc.alloc_semaphore(f"q_{E.name}_{key}")

    with nc.Block() as blk:
        @blk.sync
        def _(sync):
            for f in E_sy.prog:
                f(sync)

        @blk.gpsimd
        def _(g):
            for f in E_gp.prog:
                f(g)

        @blk.vector
        def _(v):
            for f in E_v.prog:
                f(v)

        @blk.scalar
        def _(s):
            for f in E_s.prog:
                f(s)

        @blk.tensor
        def _(t):
            for f in E_pe.prog:
                f(t)

    return nc


def _prep_host(inputs, T):
    nch = T // CH
    x = np.ascontiguousarray(inputs["L0_attn_x"])
    B = x.shape[0]

    def wlay(Wt):
        d, f = Wt.shape
        return np.ascontiguousarray(Wt.reshape(d // P, P, f).transpose(1, 0, 2)
                                    .reshape(P, (d // P) * f)).astype(np.float32)

    wq_h = wlay(np.ascontiguousarray(inputs["L0_attn_W_Q"].T))
    wk_h = wlay(np.ascontiguousarray(inputs["L0_attn_W_K"].T))
    wv_h = wlay(np.ascontiguousarray(inputs["L0_attn_W_V"].T))
    wg_h = wlay(np.ascontiguousarray(inputs["L0_attn_W_g_w"].T))
    wo_h = wlay(np.ascontiguousarray(inputs["L0_attn_W_O"].T))
    fc_h = wlay(np.ascontiguousarray(inputs["L0_ffn_W_fc"].T))
    pj_h = wlay(np.ascontiguousarray(inputs["L0_ffn_W_proj"].T))
    bg_h = np.ascontiguousarray(np.asarray(inputs["L0_attn_W_g_b"], np.float32)
                                .reshape(4, P).T)
    dl_h = np.ascontiguousarray(np.asarray(inputs["L0_attn_decay_log"],
                                           np.float32).reshape(1, H))
    tt = np.arange(P)
    consts = {
        "identb": np.eye(P).astype(ml_dtypes.bfloat16),
        "tsc": np.maximum(tt[None, :] - tt[:, None], 0).astype(ml_dtypes.bfloat16),
        "zmask": (tt[None, :] >= tt[:, None]).astype(ml_dtypes.bfloat16),
        "r1": np.broadcast_to(tt[None, :] + 1, (P, P)).astype(ml_dtypes.bfloat16),
        "r127": (127 - tt)[:, None].astype(np.float32),
        "onesr": np.ones((1, P), np.float32),
        "onescb": np.ones((P, 1), ml_dtypes.bfloat16),
    }
    selm = np.zeros((H, FH), ml_dtypes.bfloat16)
    for cfull in range(FH):
        selm[cfull // DKH, cfull] = 1.0
    consts["sel"] = selm

    in_maps = []
    for b in range(B):
        x_tok = np.ascontiguousarray(x[b].reshape(nch, P, D)).astype(np.float32)
        m = {"x_tok": x_tok,
             "wq": wq_h, "wk": wk_h, "wv": wv_h, "wg": wg_h, "wo": wo_h,
             "fc": fc_h, "pj": pj_h, "bg": bg_h, "dl": dl_h}
        m.update(consts)
        in_maps.append(m)
    return in_maps


def kernel(**inputs):
    T = inputs["L0_attn_x"].shape[1]
    B = inputs["L0_attn_x"].shape[0]
    nc = build_program(T)
    in_maps = _prep_host(inputs, T)
    res = run_bass_kernel_spmd(nc, in_maps, core_ids=list(range(B)))
    out = np.stack([res.results[b]["out"].reshape(T, D) for b in range(B)], axis=0)
    return out.astype(np.float32)


# version 15 (final)
# speedup vs baseline: 1.0770x; 1.0770x over previous
import numpy as np
import ml_dtypes
import concourse.bass as bass
import concourse.mybir as mybir
from concourse.bass_utils import run_bass_kernel_spmd

F32 = mybir.dt.float32
BF16 = mybir.dt.bfloat16
AF = mybir.ActivationFunctionType
ALU = mybir.AluOpType
AXL = mybir.AxisListType

P = 128
D = 1024
FH = 512
H = 8
DKH = 64
FF = 4096
CH = 128
ST = 2
EPS_LN = 1e-5
EPS_AT = 1e-6
THS = [0.24419428096993973, 0.4639691338428854, 0.6861859295255306,
       0.9059607823984763, 1.0839784132255623]


class Eng:
    def __init__(self, name, no_drain=False):
        self.name = name
        self.no_drain = no_drain
        self.prog = []
        self.sem = None
        self.count = 0
        self.dirty_w = set()
        self.dirty_r = set()

    def drain(self):
        self.prog.append(lambda e: e.drain())
        self.dirty_w.clear()
        self.dirty_r.clear()

    def op(self, fn, reads=(), writes=(), inc=False):
        hazard = (not self.no_drain) and (
            any(t in self.dirty_w for t in reads) or
            any((t in self.dirty_w or t in self.dirty_r) for t in writes))
        if hazard:
            self.drain()
        if inc:
            self.count += 1
            me = self

            def run(e, fn=fn):
                fn(e).then_inc(me.sem, 1)
            self.prog.append(run)
        else:
            self.prog.append(lambda e, fn=fn: fn(e))
        self.dirty_r.update(reads)
        self.dirty_w.update(writes)
        return self.count

    def dma(self, dst_ap, src_ap, key="d"):
        if not hasattr(self, "dsem"):
            self.dsem = {}
            self.dcnt = {}
        if key not in self.dcnt:
            self.dcnt[key] = 0
            self.dsem[key] = None
        self.dcnt[key] += 1
        me = self

        def run(e, d=dst_ap, s=src_ap, key=key):
            e.dma_start(out=d, in_=s).then_inc(me.dsem[key], 16)
        self.prog.append(run)
        return (key, self.dcnt[key])

    def wait(self, other, value, dma=False):
        if dma:
            key, n = value
            if n <= 0:
                return
            o = other
            self.prog.append(lambda e: e.wait_ge(o.dsem[key], n * 16))
        else:
            if value <= 0:
                return
            o = other
            self.prog.append(lambda e: e.wait_ge(o.sem, value))


def build_program(T):
    nch = T // CH
    nst = nch // ST
    assert nch % ST == 0 and nch >= 4
    nc = bass.Bass("TRN2", target_bir_lowering=False)

    dt_ = nc.dram_tensor
    x_tok_d = dt_("x_tok", [nch, P, D], F32, kind="ExternalInput").ap()
    wq_d = dt_("wq", [P, 8 * FH], F32, kind="ExternalInput").ap()
    wk_d = dt_("wk", [P, 8 * FH], F32, kind="ExternalInput").ap()
    wv_d = dt_("wv", [P, 8 * FH], F32, kind="ExternalInput").ap()
    wg_d = dt_("wg", [P, 8 * FH], F32, kind="ExternalInput").ap()
    wo_d = dt_("wo", [P, 4 * D], F32, kind="ExternalInput").ap()
    fc_d = dt_("fc", [P, 8 * FF], F32, kind="ExternalInput").ap()
    pj_d = dt_("pj", [P, 32 * D], F32, kind="ExternalInput").ap()
    bg_d = dt_("bg", [P, 4], F32, kind="ExternalInput").ap()
    dl_d = dt_("dl", [1, H], F32, kind="ExternalInput").ap()
    identb_d = dt_("identb", [P, P], BF16, kind="ExternalInput").ap()
    tsc_d = dt_("tsc", [P, P], BF16, kind="ExternalInput").ap()
    zmask_d = dt_("zmask", [P, P], BF16, kind="ExternalInput").ap()
    r1_d = dt_("r1", [P, P], BF16, kind="ExternalInput").ap()
    r127_d = dt_("r127", [P, 1], F32, kind="ExternalInput").ap()
    sel_d = dt_("sel", [H, FH], BF16, kind="ExternalInput").ap()
    onesr_d = dt_("onesr", [1, P], F32, kind="ExternalInput").ap()
    onescb_d = dt_("onescb", [P, 1], BF16, kind="ExternalInput").ap()
    out_d = dt_("out", [nch, P, D], F32, kind="ExternalOutput").ap()

    S_ = nc.alloc_sbuf_tensor
    wq = S_("wq_sb", [P, 8 * FH], BF16)
    wk = S_("wk_sb", [P, 8 * FH], BF16)
    wv = S_("wv_sb", [P, 8 * FH], BF16)
    wg = S_("wg_sb", [P, 8 * FH], BF16)
    wo = S_("wo_sb", [P, 4 * D], BF16)
    fc = S_("fc_sb", [P, 8 * FF], BF16)
    pj = S_("pj_sb", [P, 32 * D], BF16)
    XR = S_("XR", [P, 8 * ST * CH], BF16)
    XT = [S_(f"XT{i}", [P, D], BF16) for i in range(2)]
    XF = S_("XF", [P, D], BF16)
    G = S_("G", [P, ST * D], BF16)
    MEM = S_("MEM", [P, D], F32)
    SGN = S_("SGN", [P, FH], BF16)
    SGN2 = S_("SGN2", [P, FH], BF16)
    QF = S_("QF", [P, FH], BF16)
    KF = S_("KF", [P, FH], BF16)
    THF = S_("THF", [P, FH], BF16)
    VT = S_("VT", [P, FH], BF16)
    VAUG = S_("VAUG", [P, H * 65], BF16)
    AT = S_("AT", [P, H * P], BF16)
    ZP = S_("ZP", [P, 2 * FH], BF16)
    SCRB = S_("SCRB", [P, FH], BF16)
    H2 = [S_(f"H2_{i}", [P, ST * CH], BF16) for i in range(2)]
    OUTB = MEM
    SSB = S_("SSB", [P, 4 * 65], BF16)
    GDEC = S_("GDEC", [P, 4 * 65], BF16)
    DMASK = S_("DMASK", [P, H * P], BF16)
    GQF = S_("GQF", [P, FH], BF16)
    KGM = S_("KGM", [P, FH], BF16)
    identb = S_("identb_sb", [P, P], BF16)
    tsc = G[:, 0:P]
    zmask = G[:, P:2 * P]
    r1t = G[:, 2 * P:3 * P]
    r127 = S_("r127_sb", [P, 1], F32)
    sel = S_("sel_sb", [H, FH], BF16)
    onesr = S_("onesr_sb", [1, P], F32)
    onescb = S_("onescb_sb", [P, 1], BF16)
    bgh = S_("bgh", [P, 4], F32)
    dls = S_("dls", [1, H], F32)
    lgn = S_("lgn", [1, H], F32)
    NLGB = S_("NLGB", [P, H], F32)
    NLGF = S_("NLGF", [P, 4], F32)
    G128B = S_("G128B", [P, H], F32)
    KGC = S_("KGC", [P, H], F32)
    REC = S_("REC", [P, H], F32)
    RECB = S_("RECB", [P, H], BF16)
    stats = S_("stats", [P, 16], F32)
    CONSTC = S_("CONSTC", [P, 10], F32)
    RECT = SCRB[0:H, 0:P]
    ONE64B = SCRB[:, 0:65]

    PS_ = nc.alloc_psum_tensor
    pb = [PS_(f"pb{i}", [P, FH], F32) for i in range(7)]
    pbT = PS_("pbT", [P, 2 * FH], BF16)

    E_pe = Eng("pe", no_drain=True)
    E_v, E_s, E_sy, E_gp = Eng("v"), Eng("s"), Eng("sy"), Eng("gp")

    def mm(out_ap, lhsT_ap, rhs_ap, start, stop, reads=(), writes=(), inc=False, tp=None):
        def f(e):
            return e.matmul(out_ap, lhsT_ap, rhs_ap, start=start, stop=stop,
                            tile_position=tp)
        return E_pe.op(f, reads=reads, writes=writes, inc=inc)

    for w_sb, w_dr, wkey in [(wq, wq_d, "lq"), (wk, wk_d, "lk"),
                             (wv, wv_d, "lv"), (wg, wg_d, "lg"),
                             (wo, wo_d, "lo")]:
        E_gp.dma(w_sb[:], w_dr, wkey)
    E_gp.dma(fc[:], fc_d, "wt2")
    E_gp.dma(pj[:], pj_d, "wt2")

    for dst, src in [(identb[:], identb_d), (tsc, tsc_d), (zmask, zmask_d),
                     (r1t, r1_d), (r127[:], r127_d), (sel[:], sel_d),
                     (onesr[:], onesr_d), (onescb[:], onescb_d),
                     (bgh[:], bg_d), (dls[:], dl_d)]:
        n_cst = E_sy.dma(dst, src, "cst")

    E_v.wait(E_sy, n_cst, dma=True)
    E_s.wait(E_sy, n_cst, dma=True)

    c = E_s.op(lambda e: e.activation(lgn[:], dls[:], AF.Exp, scale=-1.0),
               reads=("dls",), writes=("lgn",), inc=True)
    E_v.wait(E_s, c)
    c = E_v.op(lambda e: e.tensor_scalar_add(out=lgn[:], in0=lgn[:], scalar1=1.0),
               reads=("lgn",), writes=("lgn",), inc=True)
    E_s.wait(E_v, c)
    c_ln = E_s.op(lambda e: e.activation(lgn[:], lgn[:], AF.Ln),
                  reads=("lgn",), writes=("lgn",), inc=True)
    E_v.op(lambda e: e.tensor_scalar_mul(out=bgh[:], in0=bgh[:], scalar1=0.5),
           reads=("bgh",), writes=("bgh",))
    E_v.op(lambda e: e.memset(CONSTC[:, 0:1], -1.0), writes=("CONSTC",))
    E_v.op(lambda e: e.memset(CONSTC[:, 1:2], -6.0), writes=("CONSTC",))
    E_v.op(lambda e: e.memset(CONSTC[:, 2:3], 0.25), writes=("CONSTC",))
    E_v.op(lambda e: e.memset(CONSTC[:, 3:4], -12.0), writes=("CONSTC",))
    E_v.op(lambda e: e.memset(CONSTC[:, 4:5], 0.0), writes=("CONSTC",))
    E_v.op(lambda e: e.memset(CONSTC[:, 5:6], -THS[0]), writes=("CONSTC",))
    E_v.op(lambda e: e.memset(CONSTC[:, 6:7], -THS[1]), writes=("CONSTC",))
    E_v.op(lambda e: e.memset(CONSTC[:, 7:8], -THS[2]), writes=("CONSTC",))
    E_v.op(lambda e: e.memset(CONSTC[:, 8:9], -THS[3]), writes=("CONSTC",))
    E_v.op(lambda e: e.memset(CONSTC[:, 9:10], -THS[4]), writes=("CONSTC",))
    E_v.op(lambda e: e.memset(ZP[:], 0.0), writes=("ZP",))
    E_v.op(lambda e: e.memset(ONE64B, 1.0), writes=("SCRB",))
    c_ssb0 = E_v.op(lambda e: e.memset(SSB[:], 0.0), writes=("SSB",), inc=True)

    E_pe.wait(E_s, c_ln)
    c = mm(pb[5][0:P, 0:H], onesr[:], lgn[:], True, True,
           reads=("onesr", "lgn"), writes=("pb5",), inc=True)
    E_v.wait(E_pe, c)
    c_nl = E_v.op(lambda e: e.tensor_scalar_mul(out=NLGB[:], in0=pb[5][0:P, 0:H],
                                                scalar1=-1.0),
                  reads=("pb5",), writes=("NLGB",), inc=True)
    E_s.wait(E_v, c_nl)
    c_g128 = E_s.op(lambda e: e.activation(G128B[:], NLGB[:], AF.Exp, scale=128.0),
                    reads=("NLGB",), writes=("G128B",), inc=True)
    for h in range(H):
        c_kgc = E_s.op(lambda e, h=h: e.activation(KGC[:, h:h + 1], r127[:], AF.Exp,
                                                   scale=NLGB[:, h:h + 1]),
                       reads=("r127", "NLGB"), writes=("KGC",), inc=(h == H - 1))
    for h in range(H):
        c_dm = E_s.op(lambda e, h=h: e.activation(DMASK[:, h * P:(h + 1) * P], tsc,
                                                  AF.Exp, scale=NLGB[:, h:h + 1]),
                      reads=("G", "NLGB"), writes=("DMASK",), inc=(h == H - 1))
    for ft in range(4):
        E_v.op(lambda e, ft=ft: e.tensor_copy(out=NLGF[0:DKH, ft:ft + 1],
                                              in_=NLGB[0:DKH, 2 * ft:2 * ft + 1]),
               reads=("NLGB",), writes=("NLGF",))
        c_nlgf = E_v.op(lambda e, ft=ft: e.tensor_copy(
            out=NLGF[DKH:P, ft:ft + 1], in_=NLGB[DKH:P, 2 * ft + 1:2 * ft + 2]),
            reads=("NLGB",), writes=("NLGF",), inc=(ft == 3))
    E_s.wait(E_v, c_nlgf)
    for ft in range(4):
        c_gqf = E_s.op(lambda e, ft=ft: e.activation(GQF[:, ft * P:(ft + 1) * P], r1t,
                                                     AF.Exp, scale=NLGF[:, ft:ft + 1]),
                       reads=("G", "NLGF"), writes=("GQF",), inc=(ft == 3))
    E_v.wait(E_s, c_dm)
    for h in range(H):
        E_v.op(lambda e, h=h: e.tensor_tensor(out=DMASK[:, h * P:(h + 1) * P],
                                              in0=DMASK[:, h * P:(h + 1) * P],
                                              in1=zmask, op=ALU.mult),
               reads=("DMASK", "G"), writes=("DMASK",))
    E_v.wait(E_s, c_kgc)
    for h in range(H):
        E_v.op(lambda e, h=h: e.tensor_scalar_mul(out=KGM[:, h * DKH:(h + 1) * DKH],
                                                  in0=ONE64B[:, 0:DKH],
                                                  scalar1=KGC[:, h:h + 1]),
               reads=("SCRB", "KGC"), writes=("KGM",))
        E_v.op(lambda e, h=h: e.tensor_copy(out=VAUG[:, h * 65 + 64:h * 65 + 65],
                                            in_=KGC[:, h:h + 1]),
               reads=("KGC",), writes=("VAUG",))
    E_v.wait(E_s, c_g128)
    for p_ in range(4):
        for r_ in range(2):
            h = 2 * p_ + r_
            E_v.op(lambda e, p_=p_, r_=r_, h=h: e.tensor_scalar_mul(
                out=GDEC[r_ * DKH:(r_ + 1) * DKH, p_ * 65:(p_ + 1) * 65],
                in0=ONE64B[r_ * DKH:(r_ + 1) * DKH, 0:65],
                scalar1=G128B[r_ * DKH:(r_ + 1) * DKH, h:h + 1]),
                reads=("SCRB", "G128B"), writes=("GDEC",), inc=(p_ == 3 and r_ == 1))

    E_pe.wait(E_gp, ("lq", 1), dma=True)
    E_pe.wait(E_v, c_ssb0)

    xt_cnt = {}
    xnact_done = {}
    xnte_done = {}
    xntp_done = {}
    xnstat_done = {}
    gp_guard = {"s": 0, "g": 0}

    def load_xt(ch):
        E_gp.wait(E_s, xnact_done.get(ch - 2, 0))
        xt_cnt[ch] = E_gp.dma(XT[ch % 2][:], x_tok_d[ch], f"xt{ch % 2}")

    def emit_A_stats(ch):
        buf = XT[ch % 2]
        key = f"XT{ch % 2}"
        st_ = stats
        rcol = 8 if ch % 2 == 0 else 14
        ncol = 9 if ch % 2 == 0 else 15
        E_gp.wait(E_gp, xt_cnt[ch], dma=True)
        E_s.wait(E_gp, xt_cnt[ch], dma=True)
        E_s.wait(E_gp, gp_guard["g"])
        E_s.op(lambda e, b=buf: e.activation(XF[:], b[:], AF.Identity,
                                             accum_out=st_[:, 0:1]),
               reads=(key,), writes=("XF",))
        E_s.op(lambda e, b=buf: e.activation(XF[:, 0:FH], b[:, 0:FH], AF.Square,
                                             accum_out=st_[:, 1:2]),
               reads=(key,), writes=("XF",))
        c_sq = E_s.op(lambda e, b=buf: e.activation(XF[:, 0:FH], b[:, FH:D], AF.Square,
                                                    accum_out=st_[:, 2:3]),
                      reads=(key,), writes=("XF",), inc=True)
        E_gp.wait(E_s, c_sq)
        E_gp.wait(E_s, gp_guard["s"])
        g = E_gp
        g.op(lambda e: e.tensor_scalar_mul(out=st_[:, 12:13], in0=st_[:, 0:1],
                                           scalar1=1.0 / D),
             reads=("stats",), writes=("stats",))
        g.op(lambda e: e.tensor_scalar_mul(out=st_[:, 13:14], in0=st_[:, 0:1],
                                           scalar1=-1.0 / D),
             reads=("stats",), writes=("stats",))
        g.op(lambda e: e.tensor_tensor(out=st_[:, 3:4], in0=st_[:, 1:2],
                                       in1=st_[:, 2:3], op=ALU.add),
             reads=("stats",), writes=("stats",))
        g.op(lambda e: e.tensor_scalar(out=st_[:, 3:4], in0=st_[:, 3:4],
                                       scalar1=1.0 / D, scalar2=EPS_LN,
                                       op0=ALU.mult, op1=ALU.add),
             reads=("stats",), writes=("stats",))
        g.op(lambda e: e.tensor_tensor(out=st_[:, 4:5], in0=st_[:, 12:13],
                                       in1=st_[:, 13:14], op=ALU.mult),
             reads=("stats",), writes=("stats",))
        g.op(lambda e: e.tensor_tensor(out=st_[:, 5:6], in0=st_[:, 4:5],
                                       in1=st_[:, 3:4], op=ALU.add),
             reads=("stats",), writes=("stats",))
        g.op(lambda e: e.tensor_scalar(out=st_[:, 6:7], in0=st_[:, 5:6],
                                       scalar1=-0.5, scalar2=1.5,
                                       op0=ALU.mult, op1=ALU.add),
             reads=("stats",), writes=("stats",))
        for _ in range(3):
            g.op(lambda e: e.tensor_tensor(out=st_[:, 7:8], in0=st_[:, 6:7],
                                           in1=st_[:, 6:7], op=ALU.mult),
                 reads=("stats",), writes=("stats",))
            g.op(lambda e: e.tensor_tensor(out=st_[:, 7:8], in0=st_[:, 7:8],
                                           in1=st_[:, 5:6], op=ALU.mult),
                 reads=("stats",), writes=("stats",))
            g.op(lambda e: e.tensor_scalar(out=st_[:, 7:8], in0=st_[:, 7:8],
                                           scalar1=-0.5, scalar2=1.5,
                                           op0=ALU.mult, op1=ALU.add),
                 reads=("stats",), writes=("stats",))
            g.op(lambda e: e.tensor_tensor(out=st_[:, 6:7], in0=st_[:, 6:7],
                                           in1=st_[:, 7:8], op=ALU.mult),
                 reads=("stats",), writes=("stats",))
        g.op(lambda e: e.tensor_scalar_mul(out=st_[:, rcol:rcol + 1],
                                           in0=st_[:, 6:7], scalar1=1.0),
             reads=("stats",), writes=("stats",))
        g.op(lambda e: e.tensor_tensor(out=st_[:, 10:11], in0=st_[:, 12:13],
                                       in1=st_[:, rcol:rcol + 1], op=ALU.mult),
             reads=("stats",), writes=("stats",))
        c_neg = g.op(lambda e: e.tensor_scalar_mul(out=st_[:, ncol:ncol + 1],
                                                   in0=st_[:, 10:11], scalar1=-1.0),
                     reads=("stats",), writes=("stats",), inc=True)
        gp_guard["g"] = c_neg
        xnstat_done[ch] = c_neg

    def emit_A_xn(ch, cc):
        buf = XT[ch % 2]
        key = f"XT{ch % 2}"
        st_ = stats
        rcol = 8 if ch % 2 == 0 else 14
        ncol = 9 if ch % 2 == 0 else 15
        E_s.wait(E_gp, xnstat_done[ch])
        E_s.wait(E_pe, gp_guard.get("tp", 0))
        c_xn = E_s.op(lambda e, b=buf, rcol=rcol, ncol=ncol: e.activation(
            XF[:], b[:], AF.Identity, scale=st_[:, rcol:rcol + 1],
            bias=st_[:, ncol:ncol + 1]),
            reads=(key, "stats"), writes=("XF",), inc=True)
        xnact_done[ch] = c_xn
        gp_guard["s"] = c_xn
        E_pe.wait(E_s, c_xn)
        E_pe.wait(E_v, gp_guard.get("te", 0))
        for dk in range(8):
            c_T = E_pe.op(lambda e, dk=dk: e.transpose(pbT[:, dk * P:(dk + 1) * P],
                                                       XF[:, dk * P:(dk + 1) * P],
                                                       identb[:]),
                          reads=("XF", "identb"), writes=("pbT",), inc=(dk == 7))
        xntp_done[ch] = c_T
        gp_guard["tp"] = c_T
        E_v.wait(E_pe, c_T)
        xr3 = XR[:].rearrange("p (a b) -> p a b", b=ST * CH)[:, :, cc * CH:(cc + 1) * CH]
        pt3 = pbT[:, 0:8 * P].rearrange("p (a b) -> p a b", b=P)
        c_ev = E_v.op(lambda e, xr3=xr3, pt3=pt3: e.tensor_copy(
            out=xr3, in_=pt3),
            reads=("pbT",), writes=("XR",), inc=True)
        xnte_done[ch] = c_ev
        gp_guard["te"] = c_ev

    load_xt(0)
    load_xt(1)
    emit_A_stats(0)
    emit_A_stats(1)
    emit_A_xn(0, 0)
    emit_A_xn(1, 1)

    combine_done = {-1: 0, -2: 0}
    combine_mul = {-1: 0, -2: 0}
    store_half = {}
    c_vtab = E_v.count
    th_done = {}
    vaug_done = {}
    ssb_done = {-1: c_ssb0}
    yb_done = {-1: c_vtab, -2: 0}
    wo_done = {-1: 0}
    ptail = {0: (0, 0, 0), 1: (0, 0, 0)}
    yq_free_v = 0
    yq_free_s = 0
    yev0 = 0
    yev1 = 0
    y2ev = {}
    pending_c1 = None

    for sti in range(nst):
        c0 = sti * ST
        chunks = [c0, c0 + 1]
        for cc, ch in enumerate(chunks):
            qbank, kbank = pb[2], pb[3]
            if sti == 0 and cc == 0:
                E_pe.wait(E_gp, ("lq", 1), dma=True)
            if cc == 0:
                E_pe.wait(E_v, combine_mul[c0 - 2])
                E_pe.wait(E_v, xnte_done.get(ch, 0))
            else:
                E_pe.wait(E_s, th_done[ch - 1])
                E_pe.wait(E_v, vaug_done[ch - 1])
                E_pe.wait(E_v, xnte_done.get(ch, 0))
            for ft in range(4):
                for dk in range(8):
                    c_qps = mm(qbank[:, ft * P:(ft + 1) * P],
                               wq[:, dk * FH + ft * P:dk * FH + (ft + 1) * P],
                               XR[:, dk * ST * CH + cc * CH:dk * ST * CH + cc * CH + CH],
                               start=(dk == 0), stop=(dk == 7),
                               reads=("wq_sb", "XR"), writes=("pb2",),
                               inc=(ft == 3 and dk == 7))
            E_s.wait(E_pe, c_qps)
            E_s.wait(E_pe, wo_done.get(ch - 1, 0))
            E_s.wait(E_v, yb_done.get(ch - 1, 0))
            if cc == 0:
                E_s.wait(E_v, yq_free_v)
            E_s.op(lambda e: e.activation(QF[:], qbank[:, 0:FH], AF.Relu, scale=-1.0),
                   reads=("pb2",), writes=("QF",))
            c_qph = E_s.op(lambda e: e.activation(SCRB[:], QF[:], AF.Exp, scale=-1.0),
                           reads=("QF",), writes=("SCRB",), inc=True)
            E_v.wait(E_pe, c_qps)
            E_v.wait(E_s, c_qph)
            c_qf = E_v.op(lambda e: e.scalar_tensor_tensor(
                out=QF[:], in0=qbank[:, 0:FH], scalar=0.0, in1=SCRB[:],
                op0=ALU.max, op1=ALU.add),
                reads=("pb2", "SCRB"), writes=("QF",), inc=True)
            if sti == 0 and cc == 0:
                E_pe.wait(E_gp, ("lk", 1), dma=True)
            for ft in range(4):
                for dk in range(8):
                    c_kps = mm(kbank[:, ft * P:(ft + 1) * P],
                               wk[:, dk * FH + ft * P:dk * FH + (ft + 1) * P],
                               XR[:, dk * ST * CH + cc * CH:dk * ST * CH + cc * CH + CH],
                               start=(dk == 0), stop=(dk == 7),
                               reads=("wk_sb", "XR"), writes=("pb3",),
                               inc=(ft == 3 and dk == 7))
            E_s.wait(E_pe, c_kps)
            E_s.wait(E_v, c_qf)
            E_s.op(lambda e: e.activation(KF[:], kbank[:, 0:FH], AF.Relu, scale=-1.0),
                   reads=("pb3",), writes=("KF",))
            c_kph = E_s.op(lambda e: e.activation(SCRB[:], KF[:], AF.Exp, scale=-1.0),
                           reads=("KF",), writes=("SCRB",), inc=True)
            E_v.wait(E_pe, c_kps)
            E_v.wait(E_s, c_kph)
            c_kf = E_v.op(lambda e: e.scalar_tensor_tensor(
                out=KF[:], in0=kbank[:, 0:FH], scalar=0.0, in1=SCRB[:],
                op0=ALU.max, op1=ALU.add),
                reads=("pb3", "SCRB"), writes=("KF",), inc=True)
            E_v.op(lambda e: e.tensor_copy(out=ZP[0:DKH, 0:FH], in_=KF[0:DKH, 0:FH]),
                   reads=("KF",), writes=("ZP",))
            c_kz = E_v.op(lambda e: e.tensor_copy(out=ZP[DKH:P, FH:2 * FH],
                                                  in_=KF[DKH:P, 0:FH]),
                          reads=("KF",), writes=("ZP",), inc=True)
            if cc == 0 and pending_c1 is not None:
                pending_c1()
                gp_guard["g"] = combine_done[c0 - 1]
                pending_c1 = None
            if sti == 0 and cc == 0:
                E_pe.wait(E_gp, ("lg", 1), dma=True)
            E_pe.wait(E_s, c_qph)
            E_pe.wait(E_v, c_qf)
            for ft in range(4):
                for dk in range(8):
                    c_gps = mm(qbank[:, ft * P:(ft + 1) * P],
                               wg[:, dk * FH + ft * P:dk * FH + (ft + 1) * P],
                               XR[:, dk * ST * CH + cc * CH:dk * ST * CH + cc * CH + CH],
                               start=(dk == 0), stop=(dk == 7),
                               reads=("wg_sb", "XR"), writes=("pb2",),
                               inc=(ft == 3 and dk == 7))
            E_s.wait(E_pe, c_gps)
            E_s.wait(E_v, yb_done.get(ch - 1, 0))
            for ft in range(4):
                c_th = E_s.op(lambda e, ft=ft: e.activation(
                    THF[:, ft * P:(ft + 1) * P], qbank[:, ft * P:(ft + 1) * P],
                    AF.Tanh, scale=0.5, bias=bgh[:, ft:ft + 1]),
                    reads=("pb2", "bgh"), writes=("THF",), inc=(ft == 3))
            th_done[ch] = c_th
            if sti == 0 and cc == 0:
                E_pe.wait(E_gp, ("lv", 1), dma=True)
            E_pe.wait(E_s, c_kph)
            E_pe.wait(E_v, c_kf)
            for dk in range(8):
                c_vps = mm(kbank[:, 0:FH],
                           XR[:, dk * ST * CH + cc * CH:dk * ST * CH + cc * CH + CH],
                           wv[:, dk * FH:(dk + 1) * FH],
                           start=(dk == 0), stop=(dk == 7),
                           reads=("wv_sb", "XR"), writes=("pb3",), inc=(dk == 7))
            E_v.wait(E_pe, c_vps)
            E_v.op(lambda e: e.tensor_copy(out=VT[:], in_=kbank[:, 0:FH]),
                   reads=("pb3",), writes=("VT",))
            vaug3 = VAUG[:].rearrange("p (h w) -> p h w", w=65)[:, :, 0:DKH]
            vt3 = VT[:].rearrange("p (h w) -> p h w", w=DKH)
            kg3 = KGM[:].rearrange("p (h w) -> p h w", w=DKH)
            c_vaug = E_v.op(lambda e: e.tensor_tensor(out=vaug3, in0=vt3, in1=kg3,
                                                      op=ALU.mult),
                            reads=("VT", "KGM"), writes=("VAUG",), inc=True)
            vaug_done[ch] = c_vaug
            E_pe.wait(E_v, c_kz)
            E_pe.wait(E_v, ssb_done.get(ch - 1, 0))
            if cc == 0:
                E_pe.wait(E_s, yev0)
                E_pe.wait(E_v, yev1)
                E_pe.wait(E_v, combine_mul[c0 - 1])
            for h in range(H):
                r_, ft = h % 2, h // 2
                abank = pb[h // 4]
                c_amm = mm(abank[:, (h % 4) * P:(h % 4 + 1) * P],
                           ZP[:, r_ * FH + ft * P:r_ * FH + (ft + 1) * P],
                           QF[:, ft * P:(ft + 1) * P],
                           True, True, reads=("ZP", "QF"), writes=(f"pb{h // 4}",),
                           inc=(h == H - 1))
            E_v.wait(E_pe, c_amm)
            E_v.op(lambda e: e.tensor_tensor(out=AT[:, 0:FH], in0=pb[0][:, 0:FH],
                                             in1=DMASK[:, 0:FH], op=ALU.mult),
                   reads=("pb0", "DMASK"), writes=("AT",))
            c_at = E_v.op(lambda e: e.tensor_tensor(out=AT[:, FH:2 * FH],
                                                    in0=pb[1][:, 0:FH],
                                                    in1=DMASK[:, FH:2 * FH],
                                                    op=ALU.mult),
                          reads=("pb1", "DMASK"), writes=("AT",), inc=True)
            E_v.op(lambda e: e.tensor_tensor(out=QF[:], in0=QF[:], in1=GQF[:],
                                             op=ALU.mult),
                   reads=("QF", "GQF"), writes=("QF",))
            E_v.op(lambda e: e.tensor_copy(out=ZP[0:DKH, 0:FH], in_=QF[0:DKH, 0:FH]),
                   reads=("QF",), writes=("ZP",))
            c_qz = E_v.op(lambda e: e.tensor_copy(out=ZP[DKH:P, FH:2 * FH],
                                                  in_=QF[DKH:P, 0:FH]),
                          reads=("QF",), writes=("ZP",), inc=True)
            E_pe.wait(E_v, c_at)
            if cc == 0:
                E_pe.wait(E_v, xnte_done.get(ch + 1, 0))
            for ft in range(4):
                c_kt = E_pe.op(lambda e, ft=ft: e.transpose(
                    pbT[:, ft * P:(ft + 1) * P], KF[:, ft * P:(ft + 1) * P],
                    identb[:]),
                    reads=("KF", "identb"), writes=("pbT",), inc=(ft == 3))
            opb = pb[4] if cc == 0 else pb[5]
            opkey = "pb4" if cc == 0 else "pb5"
            E_pe.wait(E_v, c_qz)
            if cc == 0:
                E_pe.wait(E_s, yq_free_s)
                E_pe.wait(E_v, yq_free_v)
            for p_ in range(4):
                for r_ in range(2):
                    h = 2 * p_ + r_
                    mm(opb[r_ * DKH:(r_ + 1) * DKH, p_ * P:(p_ + 1) * P],
                       SSB[:, p_ * 65:p_ * 65 + DKH],
                       ZP[:, r_ * FH + p_ * P:r_ * FH + (p_ + 1) * P],
                       True, False, reads=("SSB", "ZP"), writes=(opkey,),
                       tp=(0, r_ * DKH))
                    mm(opb[r_ * DKH:(r_ + 1) * DKH, p_ * P:(p_ + 1) * P],
                       VT[:, h * DKH:(h + 1) * DKH],
                       AT[:, h * P:(h + 1) * P],
                       False, True, reads=("VT", "AT"), writes=(opkey,),
                       tp=(0, r_ * DKH))
            for h in range(H):
                p_, r_ = h // 2, h % 2
                mm(pb[0][0:P, h:h + 1], AT[:, h * P:(h + 1) * P], onescb[:],
                   True, False, reads=("AT", "onescb"), writes=("pb0",))
                c_omm = mm(pb[0][0:P, h:h + 1],
                           ZP[:, r_ * FH + p_ * P:r_ * FH + (p_ + 1) * P],
                           SSB[:, p_ * 65 + DKH:p_ * 65 + 65],
                           False, True, reads=("ZP", "SSB"), writes=("pb0",),
                           inc=(h == H - 1))
            E_v.wait(E_pe, c_kt)
            c_ktok = E_v.op(lambda e: e.tensor_copy(out=KF[:], in_=pbT[:, 0:FH]),
                            reads=("pbT",), writes=("KF",), inc=True)
            E_v.wait(E_pe, c_omm)
            E_v.op(lambda e: e.tensor_scalar_add(out=REC[:], in0=pb[0][0:P, 0:H],
                                                 scalar1=EPS_AT),
                   reads=("pb0",), writes=("REC",))
            E_v.op(lambda e: e.reciprocal(out=REC[:], in_=REC[:]),
                   reads=("REC",), writes=("REC",))
            c_rec = E_v.op(lambda e: e.tensor_copy(out=RECB[:], in_=REC[:]),
                           reads=("REC",), writes=("RECB",), inc=True)
            E_pe.wait(E_v, c_rec)
            c_rt = E_pe.op(lambda e: e.transpose(pbT[0:H, FH:FH + P], RECB[:],
                                                 identb[:]),
                           reads=("RECB", "identb"), writes=("pbT",), inc=True)
            E_v.wait(E_pe, c_rt)
            c_rte = E_v.op(lambda e: e.tensor_copy(out=RECT, in_=pbT[0:H, FH:FH + P]),
                           reads=("pbT",), writes=("SCRB",), inc=True)
            E_pe.wait(E_v, c_rte)
            for ft in range(4):
                mm(pb[1][:, ft * P:(ft + 1) * P], sel[:, ft * P:(ft + 1) * P],
                   RECT, True, True, reads=("sel", "SCRB"), writes=("pb1",))
            for p_ in range(4):
                for r_ in range(2):
                    h = 2 * p_ + r_
                    ft = h // 2
                    c_stmm = mm(pb[0][r_ * DKH:(r_ + 1) * DKH,
                                      128 + p_ * 65:128 + (p_ + 1) * 65],
                                KF[:, ft * P + r_ * DKH:ft * P + (r_ + 1) * DKH],
                                VAUG[:, h * 65:(h + 1) * 65],
                                True, True, reads=("KF", "VAUG"), writes=("pb0",),
                                tp=(0, r_ * DKH), inc=(h == H - 1))
            ybbuf = SGN if cc == 0 else SCRB
            ybkey = "SGN" if cc == 0 else "SCRB"
            E_v.wait(E_pe, c_stmm)
            E_v.wait(E_s, c_th)
            if cc == 0:
                E_v.wait(E_pe, wo_done.get(ch - 2, 0))
            E_v.op(lambda e, b=opb: e.scalar_tensor_tensor(
                out=QF[:], in0=THF[:], scalar=1.0, in1=b[:, 0:FH],
                op0=ALU.add, op1=ALU.mult),
                reads=("THF", opkey), writes=("QF",))
            c_yb = E_v.op(lambda e, yb_=ybbuf: e.scalar_tensor_tensor(
                out=yb_[:, 0:FH], in0=QF[:], scalar=0.5, in1=pb[1][:, 0:FH],
                op0=ALU.mult, op1=ALU.mult),
                reads=("QF", "pb1"), writes=(ybkey,), inc=True)
            yb_done[ch] = c_yb
            E_v.op(lambda e: e.tensor_tensor(out=SSB[:], in0=SSB[:], in1=GDEC[:],
                                             op=ALU.mult),
                   reads=("SSB", "GDEC"), writes=("SSB",))
            c_ssb = E_v.op(lambda e: e.tensor_tensor(out=SSB[:], in0=SSB[:],
                                                     in1=pb[0][:, 128:388], op=ALU.add),
                           reads=("SSB", "pb0"), writes=("SSB",), inc=True)
            ssb_done[ch] = c_ssb
            if sti + 1 < nst:
                load_xt(c0 + 2 + cc)
                emit_A_stats(c0 + 2 + cc)

        relu_cnt = {}
        h2_cnt = {}
        fc_cnt = {}
        pj_cnt = {}
        ymap = {(0, 0): 4, (0, 1): 5, (1, 0): 0, (1, 1): 1}
        th_c1 = c_th
        vaug_c1 = c_vaug

        def emit_fc(ft):
            bank = pb[2 + ft % 2]
            E_pe.wait(E_v, h2_cnt.get(ft - 2, 0))
            if ft == 0:
                E_pe.wait(E_s, th_c1)
                E_pe.wait(E_v, vaug_c1)
            for dk in range(8):
                c_fc = mm(bank[:, 0:ST * CH],
                          fc[:, dk * FF + ft * P:dk * FF + (ft + 1) * P],
                          XR[:, dk * ST * CH:(dk + 1) * ST * CH],
                          start=(dk == 0), stop=(dk == 7),
                          reads=("fc_sb", "XR"), writes=(f"pb{2 + ft % 2}",),
                          inc=(dk == 7))
            fc_cnt[ft] = c_fc
            E_s.wait(E_pe, c_fc)
            E_s.wait(E_pe, pj_cnt.get(ft - 2, 0))
            relu_cnt[ft] = E_s.op(
                lambda e, b=bank, ft=ft: e.activation(H2[ft % 2][:], b[:, 0:ST * CH],
                                                      AF.Relu),
                reads=(f"pb{2 + ft % 2}",), writes=(f"H2_{ft % 2}",), inc=True)
            E_v.wait(E_s, relu_cnt[ft])
            h2_cnt[ft] = E_v.op(
                lambda e, b=bank, ft=ft: e.tensor_tensor(
                    out=H2[ft % 2][:], in0=H2[ft % 2][:], in1=b[:, 0:ST * CH],
                    op=ALU.mult),
                reads=(f"H2_{ft % 2}", f"pb{2 + ft % 2}"),
                writes=(f"H2_{ft % 2}",), inc=True)

        def emit_pj(ft):
            E_pe.wait(E_v, h2_cnt[ft])
            if ft == 0:
                E_pe.wait(E_v, ssb_done[c0 + 1])
            for tsub in range(ST):
                for doh in range(2):
                    bank = pb[ymap[(tsub, doh)]]
                    c_pj = mm(bank[:, 0:FH],
                              H2[ft % 2][:, tsub * CH:(tsub + 1) * CH],
                              pj[:, ft * D + doh * FH:ft * D + (doh + 1) * FH],
                              start=(ft == 0), stop=(ft == 31),
                              reads=(f"H2_{ft % 2}", "pj_sb"),
                              writes=(f"pb{ymap[(tsub, doh)]}",),
                              inc=(tsub == ST - 1 and doh == 1))
            pj_cnt[ft] = c_pj

        if sti == 0:
            E_pe.wait(E_gp, ("wt2", 2), dma=True)
        emit_fc(0)
        emit_fc(1)
        for ft in range(1, 32):
            emit_pj(ft - 1)
            if ft + 1 < 32:
                emit_fc(ft + 1)
        emit_pj(31)
        c_yready = pj_cnt[31]
        h2_31 = h2_cnt[31]

        if sti == 0:
            E_pe.wait(E_gp, ("lo", 1), dma=True)
        E_pe.wait(E_v, yb_done[c0])
        E_pe.wait(E_v, h2_31)
        for doh in range(2):
            for p_ in range(4):
                c_wo0 = mm(pb[2 + doh][:, 0:FH], SGN[:, p_ * P:(p_ + 1) * P],
                           wo[:, p_ * D + doh * FH:p_ * D + (doh + 1) * FH],
                           start=(p_ == 0), stop=(p_ == 3),
                           reads=("SGN", "wo_sb"), writes=(f"pb{2 + doh}",),
                           inc=(doh == 1 and p_ == 3))
        wo_done[c0] = c_wo0

        inv = {v: k for k, v in ymap.items()}
        ybanks = [4, 5, 0, 1]
        E_v.wait(E_pe, c_yready)
        for doh_ in range(2):
            E_v.wait(E_sy, store_half.get((c0 - 1, doh_), (f"st{doh_}", 0)),
                     dma=True)
        E_v.wait(E_gp, combine_done.get(c0 - 1, 0))
        E_s.wait(E_pe, wo_done[c0])

        def spike_group(piE, piO):
            ctx = {}
            for par, pi in ((0, piE), (1, piO)):
                bk = ybanks[pi]
                tsub, doh = inv[bk]
                ctx[par] = dict(
                    bk=bk, gc0=tsub * D + doh * FH,
                    MB=MEM[:, par * FH:(par + 1) * FH],
                    sgs=((SGN, "SGN"), (QF, "QF")) if par == 0 else
                        ((SGN2, "SGN2"), (KF, "KF")),
                    accb=pb[6] if par == 0 else pb[4],
                    acck="pb6" if par == 0 else "pb4")
            cE, cO = ctx[0], ctx[1]
            if piE == 0:
                ysrcE, ysrcEk = MEM[:, 0:FH], "MEM"
                c_yev = E_v.op(lambda e: e.tensor_scalar_mul(
                    out=MEM[:, 0:FH], in0=pb[4][:, 0:FH], scalar1=1.0),
                    reads=("pb4",), writes=("MEM",), inc=True)
            else:
                ysrcE, ysrcEk = pb[cE["bk"]][:, 0:FH], f"pb{cE['bk']}"
                c_yev = 0
            aE = {-2: ptail[0][2], -1: ptail[0][2]}
            aO = {-2: ptail[1][2], -1: ptail[1][2]}
            for j in range(5):
                bufE, keyE = cE["sgs"][j % 2]
                bufO, keyO = cO["sgs"][j % 2]
                E_s.wait(E_pe, aE[j - 2])
                if j == 0:
                    E_s.wait(E_v, ptail[0][0])
                    if piE == 0:
                        E_s.wait(E_v, c_yev)
                sc = E_s.op(lambda e, bufE=bufE, j=j, y_=ysrcE: e.activation(
                    bufE[:, 0:FH], y_, AF.Sign,
                    bias=CONSTC[:, 5 + j:6 + j]),
                    reads=(ysrcEk, "CONSTC"), writes=(keyE,), inc=True)
                E_v.wait(E_pe, aO[j - 2])
                vc = E_v.op(lambda e, bufO=bufO, j=j, c=cO: e.tensor_scalar(
                    out=bufO[:, 0:FH], in0=pb[c["bk"]][:, 0:FH],
                    scalar1=THS[j], scalar2=2.0, op0=ALU.is_gt, op1=ALU.mult),
                    reads=(f"pb{cO['bk']}",), writes=(keyO,), inc=True)
                E_pe.wait(E_s, sc)
                if j == 0:
                    E_pe.wait(E_s, ptail[0][1])
                    E_pe.wait(E_v, ptail[0][0])
                aE[j] = mm(cE["accb"][:, 0:FH], identb[:], bufE[:, 0:FH],
                           j == 0, j == 4, reads=("identb", keyE),
                           writes=("pb6",), inc=True)
                E_pe.wait(E_v, vc)
                if j == 0:
                    E_pe.wait(E_s, ptail[1][1])
                    E_pe.wait(E_v, ptail[1][0])
                    if piE == 0:
                        E_pe.wait(E_v, c_yev)
                aO[j] = mm(cO["accb"][:, 0:FH], identb[:], bufO[:, 0:FH],
                           j == 0, j == 4, reads=("identb", keyO),
                           writes=("pb4",), inc=True)
            y2ev[cE["bk"]] = ("s", sc)
            y2ev[cO["bk"]] = ("v", vc)
            for par in (0, 1):
                c = ctx[par]
                acc4 = aE[4] if par == 0 else aO[4]
                zb = CONSTC[:, 2:3] if par == 0 else CONSTC[:, 4:5]
                zoff = 0.25 if par == 0 else 0.0
                sgb, sgk = c["sgs"][0]
                E_s.wait(E_pe, acc4)
                thg = E_s.op(lambda e, c=c, sgb=sgb, zb=zb: e.activation(
                    sgb[:, 0:FH], c["accb"][:, 0:FH], AF.Tanh, scale=0.05,
                    bias=zb),
                    reads=(c["acck"], "CONSTC"), writes=(sgk,), inc=True)
                E_v.wait(E_s, thg)
                E_v.op(lambda e, c=c, zoff=zoff: e.tensor_scalar(
                    out=c["MB"], in0=c["accb"][:, 0:FH], scalar1=0.05,
                    scalar2=zoff, op0=ALU.mult, op1=ALU.add),
                    reads=(c["acck"],), writes=("MEM",))
                zav = E_v.op(lambda e, c=c, sgb=sgb: e.scalar_tensor_tensor(
                    out=G[:, c["gc0"]:c["gc0"] + FH], in0=sgb[:, 0:FH],
                    scalar=1.0, in1=c["MB"], op0=ALU.add, op1=ALU.mult),
                    reads=(sgk, "MEM"), writes=("G",), inc=True)
                ptail[par] = (zav, thg, acc4)

        def emit_combine(cc, ch, xguard, s_tail, g_tail):
            if xguard:
                E_gp.wait(E_pe, xguard)
            E_gp.wait(E_gp, combine_done.get(ch - 1, 0))
            xc = E_gp.dma(XF[:], x_tok_d[ch], "xc")
            E_v.wait(E_pe, wo_done[ch])
            E_v.wait(E_s, s_tail)
            E_v.wait(E_gp, g_tail)
            sbanks = ((pb[2], "pb2"), (pb[3], "pb3")) if cc == 0 else \
                     ((pb[5], "pb5"), (pb[0], "pb0"))
            E_gp.wait(E_gp, xc, dma=True)
            for doh in range(2):
                sb_, sbk = sbanks[doh]
                E_v.wait(E_sy, store_half.get((ch - 1, doh), (f"st{doh}", 0)),
                         dma=True)
                c_mul = E_v.op(lambda e, doh=doh, cc=cc, sb_=sb_: e.tensor_tensor(
                    out=OUTB[:, doh * FH:(doh + 1) * FH], in0=sb_[:, 0:FH],
                    in1=G[:, cc * D + doh * FH:cc * D + (doh + 1) * FH],
                    op=ALU.mult),
                    reads=(sbk, "G"), writes=("MEM",), inc=True)
                E_gp.wait(E_v, c_mul)
                c_comb = E_gp.op(lambda e, doh=doh: e.tensor_tensor(
                    out=OUTB[:, doh * FH:(doh + 1) * FH],
                    in0=OUTB[:, doh * FH:(doh + 1) * FH],
                    in1=XF[:, doh * FH:(doh + 1) * FH], op=ALU.add),
                    reads=("MEM", "XF"), writes=("MEM",), inc=True)
                E_sy.wait(E_gp, c_comb)
                store_half[(ch, doh)] = E_sy.dma(
                    out_d[ch][:, doh * FH:(doh + 1) * FH],
                    OUTB[:, doh * FH:(doh + 1) * FH], f"st{doh}")
            combine_mul[ch] = c_mul
            combine_done[ch] = c_comb

        def wo1_doh0():
            E_pe.wait(E_v, y2ev[5][1])
            E_pe.wait(E_v, yb_done[c0 + 1])
            for p_ in range(4):
                mm(pb[5][:, 0:FH], SCRB[:, p_ * P:(p_ + 1) * P],
                   wo[:, p_ * D:p_ * D + FH], start=(p_ == 0), stop=(p_ == 3),
                   reads=("SCRB", "wo_sb"), writes=("pb5",))

        def wo1_doh1():
            E_pe.wait(E_s, y2ev[0][1])
            for p_ in range(4):
                c_wo1 = mm(pb[0][:, 0:FH], SCRB[:, p_ * P:(p_ + 1) * P],
                           wo[:, p_ * D + FH:(p_ + 1) * D], start=(p_ == 0),
                           stop=(p_ == 3), reads=("SCRB", "wo_sb"),
                           writes=("pb0",), inc=(p_ == 3))
            wo_done[c0 + 1] = c_wo1

        spike_group(0, 1)
        wo1_doh0()
        if sti + 1 < nst:
            E_pe.wait(E_v, ssb_done[c0 + 1])
            E_pe.wait(E_v, yb_done[c0 + 1])
            emit_A_xn(c0 + 2, 0)
        spike_group(2, 3)
        wo1_doh1()
        if sti + 1 < nst:
            emit_A_xn(c0 + 3, 1)
        yq_free_v = ptail[1][0]
        yq_free_s = ptail[1][1]
        yev0 = y2ev[0][1]
        yev1 = y2ev[1][1]
        spk_s_tail = ptail[1][1]
        spk_g_tail = E_gp.count
        xg_st = xntp_done[c0 + 3] if sti + 1 < nst else 0
        emit_combine(0, c0, xg_st, spk_s_tail, spk_g_tail)
        pending_c1 = (lambda ec=emit_combine, ch=c0 + 1, xg=xg_st,
                      st_=spk_s_tail, gt=spk_g_tail: ec(1, ch, xg, st_, gt))

    if pending_c1 is not None:
        pending_c1()
        pending_c1 = None
    E_sy.wait(E_sy, store_half[(nch - 1, 0)], dma=True)
    E_sy.wait(E_sy, store_half[(nch - 1, 1)], dma=True)

    for E in (E_pe, E_v, E_s, E_sy, E_gp):
        E.sem = nc.alloc_semaphore(f"sem_{E.name}")
        if hasattr(E, "dsem"):
            for key in E.dsem:
                E.dsem[key] = nc.alloc_semaphore(f"q_{E.name}_{key}")

    with nc.Block() as blk:
        @blk.sync
        def _(sync):
            for f in E_sy.prog:
                f(sync)

        @blk.gpsimd
        def _(g):
            for f in E_gp.prog:
                f(g)

        @blk.vector
        def _(v):
            for f in E_v.prog:
                f(v)

        @blk.scalar
        def _(s):
            for f in E_s.prog:
                f(s)

        @blk.tensor
        def _(t):
            for f in E_pe.prog:
                f(t)

    return nc


def _prep_host(inputs, T):
    nch = T // CH
    x = np.ascontiguousarray(inputs["L0_attn_x"])
    B = x.shape[0]

    def wlay(Wt):
        d, f = Wt.shape
        return np.ascontiguousarray(Wt.reshape(d // P, P, f).transpose(1, 0, 2)
                                    .reshape(P, (d // P) * f)).astype(np.float32)

    wq_h = wlay(np.ascontiguousarray(inputs["L0_attn_W_Q"].T))
    wk_h = wlay(np.ascontiguousarray(inputs["L0_attn_W_K"].T))
    wv_h = wlay(np.ascontiguousarray(inputs["L0_attn_W_V"].T))
    wg_h = wlay(np.ascontiguousarray(inputs["L0_attn_W_g_w"].T))
    wo_h = wlay(np.ascontiguousarray(inputs["L0_attn_W_O"].T))
    fc_h = wlay(np.ascontiguousarray(inputs["L0_ffn_W_fc"].T))
    pj_h = wlay(np.ascontiguousarray(inputs["L0_ffn_W_proj"].T))
    bg_h = np.ascontiguousarray(np.asarray(inputs["L0_attn_W_g_b"], np.float32)
                                .reshape(4, P).T)
    dl_h = np.ascontiguousarray(np.asarray(inputs["L0_attn_decay_log"],
                                           np.float32).reshape(1, H))
    tt = np.arange(P)
    consts = {
        "identb": np.eye(P).astype(ml_dtypes.bfloat16),
        "tsc": np.maximum(tt[None, :] - tt[:, None], 0).astype(ml_dtypes.bfloat16),
        "zmask": (tt[None, :] >= tt[:, None]).astype(ml_dtypes.bfloat16),
        "r1": np.broadcast_to(tt[None, :] + 1, (P, P)).astype(ml_dtypes.bfloat16),
        "r127": (127 - tt)[:, None].astype(np.float32),
        "onesr": np.ones((1, P), np.float32),
        "onescb": np.ones((P, 1), ml_dtypes.bfloat16),
    }
    selm = np.zeros((H, FH), ml_dtypes.bfloat16)
    for cfull in range(FH):
        selm[cfull // DKH, cfull] = 1.0
    consts["sel"] = selm

    in_maps = []
    for b in range(B):
        x_tok = np.ascontiguousarray(x[b].reshape(nch, P, D)).astype(np.float32)
        m = {"x_tok": x_tok,
             "wq": wq_h, "wk": wk_h, "wv": wv_h, "wg": wg_h, "wo": wo_h,
             "fc": fc_h, "pj": pj_h, "bg": bg_h, "dl": dl_h}
        m.update(consts)
        in_maps.append(m)
    return in_maps


def kernel(**inputs):
    T = inputs["L0_attn_x"].shape[1]
    B = inputs["L0_attn_x"].shape[0]
    nc = build_program(T)
    in_maps = _prep_host(inputs, T)
    res = run_bass_kernel_spmd(nc, in_maps, core_ids=list(range(B)))
    out = np.stack([res.results[b]["out"].reshape(T, D) for b in range(B)], axis=0)
    return out.astype(np.float32)
